# version 13
# baseline (speedup 1.0000x reference)
import math
import contextlib

import numpy as np
import ml_dtypes

import concourse.bass as bass
import concourse.mybir as mybir
from concourse.alu_op_type import AluOpType as ALU
from concourse.bass_utils import run_bass_kernel_spmd

F32 = mybir.dt.float32
BF16 = mybir.dt.bfloat16
AF = mybir.ActivationFunctionType
AX = mybir.AxisListType
bfdt = ml_dtypes.bfloat16

NEG = -1.0e9


class Cfg:
    def __init__(self, S=2048, D=4096, H=16, HD=256, ROT=64, RANK=1024,
                 INTER=11008, eps=1e-5):
        self.S, self.D, self.H, self.HD, self.ROT = S, D, H, HD, ROT
        self.RANK, self.INTER, self.eps = RANK, INTER, eps
        self.DC = D // 128
        self.TT = S // 128
        self.TC = S // 512
        self.RC = RANK // 128
        self.IC = INTER // 128
        self.HC = HD // 128
        self.scale = 1.0 / math.sqrt(HD)


CFG_FULL = Cfg()



def make_consts(cfg):
    S, ROT = cfg.S, cfg.ROT
    ident = np.eye(128, dtype=bfdt)
    rt = np.zeros((128, 128), dtype=np.float32)
    for i in range(ROT // 2):
        rt[2 * i + 1, 2 * i] = -1.0
        rt[2 * i, 2 * i + 1] = 1.0
    rt = rt.astype(bfdt)
    inv_freq = 1.0 / (10000.0 ** (np.arange(0, ROT, 2, dtype=np.float32) / ROT))
    ang = inv_freq[:, None] * np.arange(S, dtype=np.float32)[None, :]
    cosx = np.ones((128, S), dtype=np.float32)
    sinx = np.zeros((128, S), dtype=np.float32)
    for d in range(ROT):
        cosx[d, :] = np.cos(ang[d // 2, :])
        sinx[d, :] = np.sin(ang[d // 2, :])
    masks = np.full((128, 5 * 512), NEG, dtype=np.float32)
    for v in range(4):
        for p in range(128):
            masks[p, v * 512: v * 512 + min(512, 128 * v + p + 1)] = 0.0
    masks[:, 4 * 512:] = 0.0
    return {"ident": ident, "rt": rt, "cosx": cosx, "sinx": sinx,
            "masks": masks}



ENGS = ("sync", "gpsimd", "tensor", "vector", "scalar")
DMA_QUEUES = ("sync", "gpsimd")


class Rec:
    __slots__ = ("eng", "emit", "reads", "writes", "dma", "idx", "deps",
                 "need_inc", "inc_val", "tag", "drain_before", "waits",
                 "semkey")

    def __init__(self, eng, emit, reads, writes, dma, idx, tag):
        self.eng, self.emit = eng, emit
        self.reads, self.writes, self.dma = tuple(reads), tuple(writes), dma
        self.idx, self.tag = idx, tag
        self.deps = set()
        self.need_inc = dma
        self.inc_val = None
        self.semkey = None
        self.drain_before = False
        self.waits = []


NLANES = 8


class Sched:
    def __init__(self, nc):
        self.nc = nc
        self.recs = []
        self.writers = {}
        self.readers = {}
        self.sems = {}
        self.counters = {e + "_cmp": 0 for e in ENGS}
        for q in DMA_QUEUES:
            for l in range(NLANES):
                self.counters[f"{q}_dma{l}"] = 0
        self.dma_n = {q: 0 for q in DMA_QUEUES}
        self.nblocks = 0

    def op(self, eng, emit, reads=(), writes=(), dma=False, tag=""):
        r = Rec(eng, emit, reads, writes, dma, len(self.recs), tag)
        if dma:
            lane = self.dma_n[eng] % NLANES
            self.dma_n[eng] += 1
            r.semkey = f"{eng}_dma{lane}"
            self.counters[r.semkey] += 16
            r.inc_val = self.counters[r.semkey]
        else:
            r.semkey = eng + "_cmp"
        for b in r.reads:
            for w in self.writers.get(b, ()):
                r.deps.add(w)
        for b in r.writes:
            for w in self.writers.get(b, ()):
                r.deps.add(w)
            for rd in self.readers.get(b, ()):
                r.deps.add(rd)
        r.deps.discard(r)
        self.recs.append(r)
        for b in r.reads:
            self.readers.setdefault(b, []).append(r)
        for b in r.writes:
            self.writers.setdefault(b, []).append(r)
        return r

    def dma(self, eng, out_ap, in_ap, reads=(), writes=(), tag=""):
        self.op(eng, (lambda o=out_ap, i=in_ap: (lambda e: e.dma_start(o, i)))(),
                reads=reads, writes=writes, dma=True, tag=tag)

    def end_block(self):
        nc = self.nc
        recs = self.recs
        pos = {}
        stream_len = {e: 0 for e in ENGS}
        for r in recs:
            pos[r] = stream_len[r.eng]
            stream_len[r.eng] += 1
        drain_cover = {e: -1 for e in ENGS}
        for r in recs:
            for d in sorted(r.deps, key=lambda x: x.idx):
                if d.dma:
                    d.need_inc = True
                    r.waits.append(d)
                elif d.eng == r.eng:
                    if r.eng == "tensor":
                        continue
                    if pos[d] > drain_cover[r.eng]:
                        r.drain_before = True
                        drain_cover[r.eng] = pos[r] - 1
                else:
                    d.need_inc = True
                    r.waits.append(d)
        for r in recs:
            if r.need_inc and not r.dma:
                self.counters[r.semkey] += 1
                r.inc_val = self.counters[r.semkey]

        sems = self.sems
        counters = dict(self.counters)

        with nc.Block(f"blk{self.nblocks}") as blk:
            for eng_name in ENGS:
                myrecs = [r for r in recs if r.eng == eng_name]
                flushes = []
                if eng_name in DMA_QUEUES:
                    for l in range(NLANES):
                        k = f"{eng_name}_dma{l}"
                        if counters[k] > 0:
                            flushes.append((sems[k], counters[k]))
                if not myrecs and not flushes:
                    continue

                def make_section(myrecs=myrecs, flushes=flushes):
                    def section(e):
                        watermark = {}
                        for r in myrecs:
                            if r.drain_before:
                                e.drain()
                            best = {}
                            for d in r.waits:
                                if best.get(d.semkey, -1) < d.inc_val:
                                    best[d.semkey] = d.inc_val
                            if r.dma and r.inc_val > 16:
                                if best.get(r.semkey, -1) < r.inc_val - 16:
                                    best[r.semkey] = r.inc_val - 16
                            for k in sorted(best):
                                v = best[k]
                                if watermark.get(k, -1) >= v:
                                    continue
                                e.wait_ge(sems[k], v)
                                watermark[k] = v
                            instr = r.emit(e)
                            if r.need_inc:
                                if instr is None:
                                    raise RuntimeError("no instr: " + r.tag)
                                instr.then_inc(sems[r.semkey],
                                               16 if r.dma else 1)
                        for (s, v) in flushes:
                            e.wait_ge(s, v)
                    return section

                getattr(blk, eng_name)(make_section())

        self.recs = []
        self.writers = {}
        self.readers = {}
        self.nblocks += 1



def build_nc(cfg, debug=False):
    nc = bass.Bass("TRN2", target_bir_lowering=False)
    S, D, H, HD, ROT, RANK, INTER = (cfg.S, cfg.D, cfg.H, cfg.HD, cfg.ROT,
                                     cfg.RANK, cfg.INTER)
    DC, TT, TC, RC, IC, HC = cfg.DC, cfg.TT, cfg.TC, cfg.RC, cfg.IC, cfg.HC

    x_d = nc.dram_tensor("x", [S, D], F32, kind="ExternalInput")
    w_attn = {n: nc.dram_tensor(n, [D, D], F32, kind="ExternalInput")
              for n in ("wq", "wk", "wv", "wo")}
    u1_d = nc.dram_tensor("u1", [D, RANK], F32, kind="ExternalInput")
    v1_d = nc.dram_tensor("v1", [RANK, 2 * INTER], F32, kind="ExternalInput")
    u2_d = nc.dram_tensor("u2", [INTER, RANK], F32, kind="ExternalInput")
    v2_d = nc.dram_tensor("v2", [RANK, D], F32, kind="ExternalInput")
    b1pt_d = nc.dram_tensor("b1pt", [128, 2 * IC], F32, kind="ExternalInput")
    b2b_d = nc.dram_tensor("b2b", [128, D], F32, kind="ExternalInput")
    ident_d = nc.dram_tensor("ident", [128, 128], BF16, kind="ExternalInput")
    rt_d = nc.dram_tensor("rt", [128, 128], BF16, kind="ExternalInput")
    cosx_d = nc.dram_tensor("cosx", [128, S], F32, kind="ExternalInput")
    sinx_d = nc.dram_tensor("sinx", [128, S], F32, kind="ExternalInput")
    masks_d = nc.dram_tensor("masks", [128, 5 * 512], F32, kind="ExternalInput")
    epsv_d = nc.dram_tensor("epsv", [128, 1], F32, kind="ExternalInput")

    out_d = nc.dram_tensor("out", [S, D], F32, kind="ExternalOutput")
    dk = dict(kind="ExternalOutput") if debug else {}
    qf_d = nc.dram_tensor("qf", [D, S], BF16, **dk)
    kf_d = nc.dram_tensor("kf", [D, S], BF16, **dk)
    vf_d = nc.dram_tensor("vf", [D, S], BF16, **dk)
    act_d = nc.dram_tensor("actsp", [IC * 128, S], BF16, **dk)
    u2bf_d = nc.dram_tensor("u2bf", [INTER, RANK], BF16)
    aout_d = nc.dram_tensor("aoutsp", [TC, DC, 128, 512], BF16, **dk)

    sc = Sched(nc)

    with contextlib.ExitStack() as ctx:
        semkeys = [e + "_cmp" for e in ENGS]
        for q in DMA_QUEUES:
            semkeys += [f"{q}_dma{l}" for l in range(NLANES)]
        for key in semkeys:
            sc.sems[key] = ctx.enter_context(nc.semaphore(key))

        id_sb = ctx.enter_context(nc.sbuf_tensor("id_sb", [128, 128], BF16))
        eps_sb = ctx.enter_context(nc.sbuf_tensor("eps_sb", [128, 1], F32))
        rt_sb = ctx.enter_context(nc.sbuf_tensor("rt_sb", [128, 128], BF16))
        b1_sb = ctx.enter_context(nc.sbuf_tensor("b1_sb", [128, 2 * IC], F32))

        st_xk = contextlib.ExitStack()
        st_cs = contextlib.ExitStack()
        st_psl = contextlib.ExitStack()
        st_z = contextlib.ExitStack()

        xk = st_xk.enter_context(nc.sbuf_tensor("xk", [128, DC * S], BF16))
        cos_sb = st_cs.enter_context(nc.sbuf_tensor("cos_sb", [128, S], F32))
        sin_sb = st_cs.enter_context(nc.sbuf_tensor("sin_sb", [128, S], F32))

        with contextlib.ExitStack() as A:
            x_in = A.enter_context(nc.sbuf_tensor("x_in", [128, 2 * D], F32))
            xn_tm = A.enter_context(nc.sbuf_tensor("xn_tm", [128, 2 * D], BF16))
            nstat = D // 512
            stats = A.enter_context(nc.sbuf_tensor("stats", [128, nstat * 6], F32))
            mv = A.enter_context(nc.sbuf_tensor("mv", [128, 2], F32))
            rstd = A.enter_context(nc.sbuf_tensor("rstd", [128, 1], F32))
            nm = A.enter_context(nc.sbuf_tensor("nm", [128, 1], F32))
            psT = A.enter_context(nc.psum_tensor("psT_A", [128, 2048], BF16))

            sc.dma("sync", id_sb[:, :], ident_d[:, :], (), ("id_sb",), "ld_id")
            sc.dma("sync", eps_sb[:, :], epsv_d[:, :], (), ("eps_sb",), "ld_eps")
            sc.dma("sync", rt_sb[:, :], rt_d[:, :], (), ("rt_sb",), "ld_rt")
            sc.dma("sync", cos_sb[:, :], cosx_d[:, :], (), ("cos_sb",), "ld_cos")
            sc.dma("sync", sin_sb[:, :], sinx_d[:, :], (), ("sin_sb",), "ld_sin")
            sc.dma("sync", b1_sb[:, :], b1pt_d[:, :], (), ("b1_sb",), "ld_b1")

            for tt in range(TT):
                p = tt % 2
                xi = x_in[:, p * D:(p + 1) * D]
                xn = xn_tm[:, p * D:(p + 1) * D]
                sc.dma("sync", xi, x_d[128 * tt:128 * (tt + 1), :],
                       (f"x_in{p}",), (f"x_in{p}",), f"ldx{tt}")
                for j in range(nstat):
                    sc.op("vector",
                          (lambda xi=xi, j=j: (lambda e: e.bn_stats(
                              stats[:, 6 * j:6 * (j + 1)],
                              xi[:, 512 * j:512 * (j + 1)])))(),
                          (f"x_in{p}",), ("stats",), tag=f"bnst{tt}.{j}")
                sc.op("vector",
                      (lambda: (lambda e: e.bn_aggr(mv[:, :], stats[:, :])))(),
                      ("stats",), ("mv",), tag=f"bnag{tt}")
                sc.op("scalar",
                      (lambda: (lambda e: e.activation(
                          rstd[:, :], mv[:, 1:2], AF.Ln,
                          bias=eps_sb[:, :])))(),
                      ("mv", "eps_sb"), ("rstd",), tag=f"lnv{tt}")
                sc.op("scalar",
                      (lambda: (lambda e: e.activation(
                          rstd[:, :], rstd[:, :], AF.Exp, scale=-0.5)))(),
                      ("rstd",), ("rstd",), tag=f"expv{tt}")
                sc.op("vector",
                      (lambda: (lambda e: e.tensor_tensor(
                          nm[:, :], mv[:, 0:1], rstd[:, :], ALU.mult)))(),
                      ("mv", "rstd"), ("nm",), tag=f"nm{tt}")
                sc.op("vector",
                      (lambda xi=xi, xn=xn: (lambda e: e.tensor_scalar(
                          xn, xi, rstd[:, :], nm[:, :],
                          op0=ALU.mult, op1=ALU.subtract)))(),
                      (f"x_in{p}", "rstd", "nm"), (f"xn_tm{p}",), tag=f"nrm{tt}")
                for g in range((DC + 3) // 4):
                    ics = list(range(4 * g, min(4 * g + 4, DC)))
                    sl = g % 2
                    def em(xn=xn, ics=ics, sl=sl):
                        def f(e):
                            ins = None
                            for k, ic in enumerate(ics):
                                ins = e.transpose(
                                    psT[:, 1024 * sl + 128 * k:
                                        1024 * sl + 128 * (k + 1)],
                                    xn[:, 128 * ic:128 * (ic + 1)], id_sb[:, :])
                            return ins
                        return f
                    sc.op("tensor", em(), (f"xn_tm{p}", "id_sb"), (f"psT_A{sl}",),
                          tag=f"xtr{tt}.{g}")
                    def ev(tt=tt, ics=ics, sl=sl):
                        def f(e):
                            return e.tensor_copy(
                                xk[:, :].rearrange("q (c s) -> q c s", c=DC)
                                [:, ics[0]:ics[0] + len(ics),
                                 128 * tt:128 * (tt + 1)],
                                psT[:, 1024 * sl:1024 * sl + 128 * len(ics)]
                                .rearrange("q (c s) -> q c s", c=len(ics)))
                        return f
                    sc.op("vector", ev(), (f"psT_A{sl}",), ("xk",),
                          tag=f"xev{tt}.{g}")
            sc.end_block()

        with contextlib.ExitStack() as B:
            wnat = B.enter_context(nc.sbuf_tensor("wnat", [128, 2 * D], BF16))
            wT = B.enter_context(nc.sbuf_tensor("wT", [128, 2 * D], BF16))
            qsb = B.enter_context(nc.sbuf_tensor("qsb", [128, 2 * S], BF16))
            qout = B.enter_context(nc.sbuf_tensor("qout", [128, 2 * S], BF16))
            trope = B.enter_context(nc.sbuf_tensor("trope", [128, 1024], F32))
            psa = B.enter_context(nc.psum_tensor("psa_B", [128, S], F32))
            psb = B.enter_context(nc.psum_tensor("psb_B", [128, 1024], F32))
            psTB = B.enter_context(nc.psum_tensor("psT_B", [128, 2048], BF16))

            half = IC * 128 // 2
            sc.dma("gpsimd", u2bf_d[0:half, :], u2_d[0:half, :],
                   (), ("U2BF",), "u2cast0")
            sc.dma("gpsimd", u2bf_d[half:, :], u2_d[half:, :],
                   (), ("U2BF",), "u2cast1")

            WLIST = [("wq", qf_d, 0), ("wk", kf_d, 1), ("wv", vf_d, 2)]
            jobs = [(wi, oc) for wi in range(3) for oc in range(DC)]

            def wload(j):
                wi, oc = jobs[j]
                p = j % 2
                sc.dma("gpsimd", wnat[:, p * D:(p + 1) * D],
                       w_attn[WLIST[wi][0]][128 * oc:128 * (oc + 1), :],
                       (f"wnat{p}",), (f"wnat{p}",), f"wld{wi}.{oc}")

            def wtrans(j):
                wi, oc = jobs[j]
                p = j % 2
                wn_ap = wnat[:, p * D:(p + 1) * D]
                wT_ap = wT[:, p * D:(p + 1) * D]
                for g in range((DC + 3) // 4):
                    ics = list(range(4 * g, min(4 * g + 4, DC)))
                    sl = g % 2
                    def em(wn_ap=wn_ap, ics=ics, sl=sl):
                        def f(e):
                            ins = None
                            for kk, ic in enumerate(ics):
                                ins = e.transpose(
                                    psTB[:, 1024 * sl + 128 * kk:
                                         1024 * sl + 128 * (kk + 1)],
                                    wn_ap[:, 128 * ic:128 * (ic + 1)],
                                    id_sb[:, :])
                            return ins
                        return f
                    sc.op("tensor", em(), (f"wnat{p}", "id_sb"),
                          (f"psT_B{sl}",), tag=f"wtr{wi}.{oc}.{g}")
                    def ev(wT_ap=wT_ap, ics=ics, sl=sl):
                        def f(e):
                            return e.tensor_copy(
                                wT_ap[:, 128 * ics[0]:
                                      128 * (ics[0] + len(ics))],
                                psTB[:, 1024 * sl:1024 * sl + 128 * len(ics)])
                        return f
                    sc.op("vector", ev(), (f"psT_B{sl}",), (f"wT{p}",),
                          tag=f"wev{wi}.{oc}.{g}")

            rope_pend = []
            rp_ctr = [0]

            def emit_rope(wi, oc, rp, p):
                qo_ap = qout[:, p * S:(p + 1) * S]
                qs_ap = qsb[:, rp * S:rp * S + S]
                for tc in range(TC):
                    bk = tc % 2
                    sc.op("tensor",
                          (lambda tc=tc, bk=bk, qs_ap=qs_ap: (lambda e:
                              e.matmul(
                                  psb[:, 512 * bk:512 * (bk + 1)],
                                  rt_sb[:, :],
                                  qs_ap[:, 512 * tc:512 * (tc + 1)],
                                  start=True, stop=True)))(),
                          ("rt_sb", f"qsb{rp}"), (f"psb_B{bk}",),
                          tag=f"rq{wi}.{oc}.{tc}")
                    sc.op("vector",
                          (lambda tc=tc, bk=bk, qs_ap=qs_ap: (lambda e:
                              e.tensor_tensor(
                                  trope[:, 512 * bk:512 * (bk + 1)],
                                  qs_ap[:, 512 * tc:512 * (tc + 1)],
                                  cos_sb[:, 512 * tc:512 * (tc + 1)],
                                  ALU.mult)))(),
                          (f"qsb{rp}", "cos_sb"), (f"trope{bk}",),
                          tag=f"rc1.{wi}.{oc}.{tc}")
                    sc.op("vector",
                          (lambda tc=tc, bk=bk, qo_ap=qo_ap: (lambda e:
                              e.scalar_tensor_tensor(
                                  qo_ap[:, 512 * tc:512 * (tc + 1)],
                                  psb[:, 512 * bk:512 * (bk + 1)], 1.0,
                                  sin_sb[:, 512 * tc:512 * (tc + 1)],
                                  op0=ALU.mult, op1=ALU.mult)))(),
                          (f"psb_B{bk}", "sin_sb"), (f"qout{p}",),
                          tag=f"rc2.{wi}.{oc}.{tc}")
                    sc.op("vector",
                          (lambda tc=tc, bk=bk, qo_ap=qo_ap: (lambda e:
                              e.tensor_tensor(
                                  qo_ap[:, 512 * tc:512 * (tc + 1)],
                                  qo_ap[:, 512 * tc:512 * (tc + 1)],
                                  trope[:, 512 * bk:512 * (bk + 1)],
                                  ALU.add)))(),
                          (f"trope{bk}", f"qout{p}"), (f"qout{p}",),
                          tag=f"rc3.{wi}.{oc}.{tc}")
                sc.dma("sync",
                       WLIST[wi][1][128 * oc:128 * (oc + 1), :], qo_ap,
                       (f"qout{p}",), (f"OF{wi}",), f"qsp{wi}.{oc}")

            wload(0)
            wload(1)
            wtrans(0)
            for j, (wi, oc) in enumerate(jobs):
                p = j % 2
                wT_ap = wT[:, p * D:(p + 1) * D]
                qo_ap = qout[:, p * S:(p + 1) * S]
                rope = (wi < 2) and (oc % HC == 0) and ROT > 0
                if j + 2 < len(jobs):
                    wload(j + 2)
                if j + 1 < len(jobs):
                    wtrans(j + 1)
                while rope_pend:
                    emit_rope(*rope_pend.pop(0))
                def mg(wT_ap=wT_ap, p=p):
                    def f(e):
                        ins = None
                        for i in range(DC - 1):
                            for tc in range(TC):
                                ins = e.matmul(
                                    psa[:, 512 * tc:512 * (tc + 1)],
                                    wT_ap[:, 128 * i:128 * (i + 1)],
                                    xk[:, i * S + 512 * tc:
                                       i * S + 512 * (tc + 1)],
                                    start=(i == 0), stop=False)
                        return ins
                    return f
                sc.op("tensor", mg(), (f"wT{p}", "xk"),
                      tuple(f"psa_B{t}" for t in range(TC)),
                      tag=f"mm{wi}.{oc}")
                for tc in range(TC):
                    sc.op("tensor",
                          (lambda wT_ap=wT_ap, tc=tc: (lambda e: e.matmul(
                              psa[:, 512 * tc:512 * (tc + 1)],
                              wT_ap[:, 128 * (DC - 1):128 * DC],
                              xk[:, (DC - 1) * S + 512 * tc:
                                 (DC - 1) * S + 512 * (tc + 1)],
                              start=(DC == 1), stop=True)))(),
                          (f"wT{p}", "xk"), (f"psa_B{tc}",),
                          tag=f"mmL{wi}.{oc}.{tc}")
                if not rope:
                    for tc in range(TC):
                        sc.op("vector",
                              (lambda qo_ap=qo_ap, tc=tc: (lambda e:
                                  e.tensor_copy(
                                      qo_ap[:, 512 * tc:512 * (tc + 1)],
                                      psa[:, 512 * tc:512 * (tc + 1)])))(),
                              (f"psa_B{tc}",), (f"qout{p}",),
                              tag=f"qev{wi}.{oc}.{tc}")
                    sc.dma("sync",
                           WLIST[wi][1][128 * oc:128 * (oc + 1), :], qo_ap,
                           (f"qout{p}",), (f"OF{wi}",), f"qsp{wi}.{oc}")
                else:
                    rp = rp_ctr[0] % 2
                    rp_ctr[0] += 1
                    qs_ap = qsb[:, rp * S:rp * S + S]
                    for tc in range(TC):
                        sc.op("vector",
                              (lambda qs_ap=qs_ap, tc=tc: (lambda e:
                                  e.tensor_copy(
                                      qs_ap[:, 512 * tc:512 * (tc + 1)],
                                      psa[:, 512 * tc:512 * (tc + 1)])))(),
                              (f"psa_B{tc}",), (f"qsb{rp}",),
                              tag=f"qevr{wi}.{oc}.{tc}")
                    rope_pend.append((wi, oc, rp, p))
            while rope_pend:
                emit_rope(*rope_pend.pop(0))
            sc.end_block()
        st_cs.close()

        psl = st_psl.enter_context(nc.sbuf_tensor("psl", [128, RC * S], BF16))
        with contextlib.ExitStack() as C:
            u1sl = C.enter_context(nc.sbuf_tensor("u1sl", [128, 2 * DC * 128],
                                                  BF16))
            psaC = C.enter_context(nc.psum_tensor("psa_C", [128, S], F32))
            for rc in range(RC):
                p = rc % 2
                u1_ap = u1sl[:, p * DC * 128:(p + 1) * DC * 128]
                sc.dma("gpsimd",
                       u1_ap.rearrange("q (n r) -> q n r", n=DC),
                       u1_d[:, 128 * rc:128 * (rc + 1)]
                       .rearrange("(n q) r -> q n r", q=128),
                       (f"u1sl{p}",), (f"u1sl{p}",), f"u1ld{rc}")
                def mg(u1_ap=u1_ap):
                    def f(e):
                        ins = None
                        for i in range(DC):
                            for tc in range(TC):
                                ins = e.matmul(
                                    psaC[:, 512 * tc:512 * (tc + 1)],
                                    u1_ap[:, 128 * i:128 * (i + 1)],
                                    xk[:, i * S + 512 * tc:i * S + 512 * (tc + 1)],
                                    start=(i == 0), stop=(i == DC - 1))
                        return ins
                    return f
                sc.op("tensor", mg(), (f"u1sl{p}", "xk"), ("psa_C",),
                      tag=f"pmm{rc}")
                sc.op("vector",
                      (lambda rc=rc: (lambda e: e.tensor_copy(
                          psl[:, rc * S:(rc + 1) * S], psaC[:, :])))(),
                      ("psa_C",), ("psl",), tag=f"pev{rc}")
            sc.end_block()

        with contextlib.ExitStack() as Dk:
            v1g = Dk.enter_context(nc.sbuf_tensor("v1g", [128, 2 * RC * 128], BF16))
            v1u = Dk.enter_context(nc.sbuf_tensor("v1u", [128, 2 * RC * 128], BF16))
            silu_f = Dk.enter_context(nc.sbuf_tensor("silu_f", [128, S], F32))
            up_f = Dk.enter_context(nc.sbuf_tensor("up_f", [128, S], F32))
            actst = Dk.enter_context(nc.sbuf_tensor("actst", [128, 2 * S], BF16))
            psg = Dk.enter_context(nc.psum_tensor("psg_D", [128, S], F32))
            psu = Dk.enter_context(nc.psum_tensor("psu_D", [128, S], F32))
            W = RC * 128
            for c in range(IC):
                p = c % 2
                g_ap = v1g[:, p * W:(p + 1) * W]
                u_ap = v1u[:, p * W:(p + 1) * W]
                a_ap = actst[:, p * S:(p + 1) * S]
                sc.dma("gpsimd", g_ap.rearrange("q (n f) -> q n f", n=RC),
                       v1_d[:, 128 * c:128 * (c + 1)]
                       .rearrange("(n q) f -> q n f", q=128),
                       (f"v1g{p}",), (f"v1g{p}",), f"v1gld{c}")
                sc.dma("gpsimd", u_ap.rearrange("q (n f) -> q n f", n=RC),
                       v1_d[:, 128 * (IC + c):128 * (IC + c + 1)]
                       .rearrange("(n q) f -> q n f", q=128),
                       (f"v1u{p}",), (f"v1u{p}",), f"v1uld{c}")
                for which, w_ap, ps, nm_ in (("g", g_ap, psg, "psg_D"),
                                             ("u", u_ap, psu, "psu_D")):
                    def mg(w_ap=w_ap, ps=ps):
                        def f(e):
                            ins = None
                            for r in range(RC):
                                for tc in range(TC):
                                    ins = e.matmul(
                                        ps[:, 512 * tc:512 * (tc + 1)],
                                        w_ap[:, 128 * r:128 * (r + 1)],
                                        psl[:, r * S + 512 * tc:
                                            r * S + 512 * (tc + 1)],
                                        start=(r == 0), stop=(r == RC - 1))
                            return ins
                        return f
                    sc.op("tensor", mg(),
                          (f"v1{which}{p}", "psl"), (nm_,), tag=f"h{which}{c}")
                sc.op("scalar",
                      (lambda c=c: (lambda e: e.activation(
                          silu_f[:, :], psg[:, :], AF.Silu,
                          bias=b1_sb[:, c:c + 1])))(),
                      ("psg_D", "b1_sb"), ("silu_f",), tag=f"sil{c}")
                sc.op("scalar",
                      (lambda c=c: (lambda e: e.activation(
                          up_f[:, :], psu[:, :], AF.Identity,
                          bias=b1_sb[:, IC + c:IC + c + 1])))(),
                      ("psu_D", "b1_sb"), ("up_f",), tag=f"up{c}")
                sc.op("vector",
                      (lambda a_ap=a_ap: (lambda e: e.tensor_tensor(
                          a_ap, silu_f[:, :], up_f[:, :], ALU.mult)))(),
                      ("silu_f", "up_f"), (f"actst{p}",), tag=f"am{c}")
                sc.dma("sync", act_d[128 * c:128 * (c + 1), :], a_ap,
                       (f"actst{p}",), ("ACT",), f"asp{c}")
            sc.end_block()
        st_psl.close()
        st_xk.close()

        zbuf = st_z.enter_context(nc.sbuf_tensor("zbuf", [128, RC * S], BF16))
        with contextlib.ExitStack() as E:
            u2row = E.enter_context(nc.sbuf_tensor("u2row", [128, 3 * RANK], BF16))
            acti = E.enter_context(nc.sbuf_tensor("acti", [128, 3 * 512], BF16))
            psz = E.enter_context(nc.psum_tensor("psz_E", [128, RC * 512], F32))
            for tc in range(TC):
                for ic in range(IC):
                    p = ic % 3
                    u2_ap = u2row[:, p * RANK:(p + 1) * RANK]
                    ac_ap = acti[:, p * 512:(p + 1) * 512]
                    sc.dma("sync", u2_ap, u2bf_d[128 * ic:128 * (ic + 1), :],
                           (f"u2row{p}",), (f"u2row{p}",), f"u2ld{tc}.{ic}")
                    sc.dma("gpsimd", ac_ap,
                           act_d[128 * ic:128 * (ic + 1),
                                 512 * tc:512 * (tc + 1)],
                           (f"acti{p}",), (f"acti{p}",), f"ald{tc}.{ic}")
                    def mg(u2_ap=u2_ap, ac_ap=ac_ap, ic=ic):
                        def f(e):
                            ins = None
                            for r in range(RC):
                                ins = e.matmul(
                                    psz[:, 512 * r:512 * (r + 1)],
                                    u2_ap[:, 128 * r:128 * (r + 1)], ac_ap,
                                    start=(ic == 0), stop=(ic == IC - 1))
                            return ins
                        return f
                    sc.op("tensor", mg(), (f"u2row{p}", f"acti{p}"), ("psz_E",),
                          tag=f"zmm{tc}.{ic}")
                sc.op("vector",
                      (lambda tc=tc: (lambda e: e.tensor_copy(
                          zbuf[:, :].rearrange("q (r s) -> q r s", r=RC)
                          [:, :, 512 * tc:512 * (tc + 1)],
                          psz[:, :].rearrange("q (r f) -> q r f", r=RC))))(),
                      ("psz_E",), ("zbuf",), tag=f"zev{tc}")
            sc.end_block()

        with contextlib.ExitStack() as Fk:
            mask_sb = Fk.enter_context(nc.sbuf_tensor("mask_sb", [128, 5 * 512], F32))
            qh = Fk.enter_context(nc.sbuf_tensor("qh", [128, 2 * HC * S], BF16))
            kh = Fk.enter_context(nc.sbuf_tensor("kh", [128, 2 * HC * S], BF16))
            vhf = Fk.enter_context(nc.sbuf_tensor("vhf", [128, 2 * HC * S], BF16))
            vtm = Fk.enter_context(nc.sbuf_tensor("vtm", [128, 2 * TT * HD], BF16))
            aoh = Fk.enter_context(nc.sbuf_tensor("aoh", [128, 2 * HC * S], BF16))
            pbuf = Fk.enter_context(nc.sbuf_tensor("pbuf", [128, 4 * S], BF16))
            SW = 512 * ((TT - 1) // 4 + 1)
            sfb_t = Fk.enter_context(nc.sbuf_tensor("sfb", [128, 2 * SW], F32))
            ptb = Fk.enter_context(nc.sbuf_tensor("ptb", [128, 2 * S], BF16))
            avtm = Fk.enter_context(nc.sbuf_tensor("avtm", [128, HD], BF16))
            mrun = Fk.enter_context(nc.sbuf_tensor("mrun", [128, 1], F32))
            bias16 = Fk.enter_context(nc.sbuf_tensor("bias16", [128, 4], F32))
            lsum = Fk.enter_context(nc.sbuf_tensor("lsum", [128, 4], F32))
            linv = Fk.enter_context(nc.sbuf_tensor("linv", [128, 4], F32))
            pss = Fk.enter_context(nc.psum_tensor("pss_F", [128, 512 * ((TT - 1) // 4 + 1)], F32))
            pspt = Fk.enter_context(nc.psum_tensor("pst_F", [128, 2048], BF16))
            psav = Fk.enter_context(nc.psum_tensor("psav_F", [128, 512], F32))
            psvt = Fk.enter_context(nc.psum_tensor("psvt_F", [128, 512], BF16))

            sc.dma("sync", mask_sb[:, :], masks_d[:, :], (), ("mask_sb",),
                   "ld_mask")
            HS = HC * S

            def head_loads(h):
                p = h % 2
                for nmq, off, src_d, bufn in (
                        (f"qh{p}", 0, qf_d, "OF0"),
                        (f"kh{p}", 0, kf_d, "OF1"),
                        (f"vhf{p}", 0, vf_d, "OF2")):
                    ap = {f"qh{p}": qh, f"kh{p}": kh, f"vhf{p}": vhf}[nmq]
                    sc.dma("sync",
                           ap[:, p * HS:(p + 1) * HS]
                           .rearrange("q (c s) -> q c s", c=HC),
                           src_d[HD * h:HD * (h + 1), :]
                           .rearrange("(c q) s -> q c s", q=128),
                           (nmq,), (nmq,), f"hld.{nmq}.{h}")

            def head_vtrans(h):
                p = h % 2
                vp = h % 2
                v_ap = vhf[:, p * HS:(p + 1) * HS]
                pairs = [(tt, hcc) for tt in range(TT) for hcc in range(HC)]
                for g in range(0, len(pairs), 4):
                    grp = pairs[g:g + 4]
                    sl = (g // 4) % 2
                    def em(v_ap=v_ap, grp=grp, sl=sl):
                        def f(e):
                            ins = None
                            for k, (tt, hcc) in enumerate(grp):
                                ins = e.transpose(
                                    pspt[:, 1024 * sl + 128 * k:
                                         1024 * sl + 128 * (k + 1)],
                                    v_ap[:, hcc * S + 128 * tt:
                                         hcc * S + 128 * (tt + 1)],
                                    id_sb[:, :])
                            return ins
                        return f
                    sc.op("tensor", em(), (f"vhf{p}", "id_sb"),
                          (f"pst_F{2 * sl}", f"pst_F{2 * sl + 1}"),
                          tag=f"vtr{h}.{g}")
                    def ev(grp=grp, sl=sl, vp=vp):
                        def f(e):
                            base = vp * TT * HD + (grp[0][0] * HC + grp[0][1]) * 128
                            return e.tensor_copy(
                                vtm[:, base:base + 128 * len(grp)],
                                pspt[:, 1024 * sl:1024 * sl + 128 * len(grp)])
                        return f
                    sc.op("vector", ev(),
                          (f"pst_F{2 * sl}", f"pst_F{2 * sl + 1}"),
                          (f"vtm{vp}",), tag=f"vev{h}.{g}")

            def head_spill(h):
                p = h % 2
                for tg in range(TC):
                    for hcc in range(HC):
                        sc.dma("sync", aout_d[tg, h * HC + hcc, :, :],
                               aoh[:, p * HS + hcc * S + 512 * tg:
                                   p * HS + hcc * S + 512 * (tg + 1)],
                               (f"aoh{p}",), ("AOUT",), f"aosp{h}.{tg}.{hcc}")

            def s1(h, qt):
                p = h % 2
                i4 = qt % 4
                sp = qt % 2
                q_ap = qh[:, p * HS:(p + 1) * HS]
                k_ap = kh[:, p * HS:(p + 1) * HS]
                nkb = qt // 4 + 1
                span = 512 * nkb
                pb = pbuf[:, i4 * S:i4 * S + S]
                sfb = sfb_t[:, sp * SW:sp * SW + SW]
                sfn = f"sfb{sp}"
                def smm(q_ap=q_ap, k_ap=k_ap, qt=qt, nkb=nkb):
                    def f(e):
                        ins = None
                        for kb in range(nkb):
                            for hcc in range(HC):
                                ins = e.matmul(
                                    pss[:, 512 * kb:512 * (kb + 1)],
                                    q_ap[:, hcc * S + 128 * qt:
                                         hcc * S + 128 * (qt + 1)],
                                    k_ap[:, hcc * S + 512 * kb:
                                         hcc * S + 512 * (kb + 1)],
                                    start=(hcc == 0), stop=(hcc == HC - 1))
                        return ins
                    return f
                sc.op("tensor", smm(), (f"qh{p}", f"kh{p}"),
                      tuple(f"pss_F{b}" for b in range(nkb)),
                      tag=f"s{h}.{qt}")
                db = nkb - 1
                sc.op("vector",
                      (lambda db=db, qt=qt, sfb=sfb: (lambda e:
                          e.scalar_tensor_tensor(
                              sfb[:, 512 * db:512 * (db + 1)],
                              pss[:, 512 * db:512 * (db + 1)], 1.0,
                              mask_sb[:, 512 * (qt % 4):512 * (qt % 4 + 1)],
                              op0=ALU.mult, op1=ALU.add)))(),
                      (f"pss_F{db}", "mask_sb"), (sfn,), tag=f"smk{h}.{qt}")
                sc.op("vector",
                      (lambda db=db, sfb=sfb: (lambda e: e.tensor_scalar(
                          sfb[:, 512 * db:512 * (db + 1)],
                          sfb[:, 512 * db:512 * (db + 1)],
                          0.0, None, op0=ALU.add, op1=ALU.max,
                          accum_out=mrun[:, :])))(),
                      (sfn,), (sfn, "mrun"), tag=f"sm0{h}.{qt}")
                for b in range(nkb - 1):
                    sc.op("vector",
                          (lambda b=b, sfb=sfb: (lambda e: e.tensor_scalar(
                              sfb[:, 512 * b:512 * (b + 1)],
                              pss[:, 512 * b:512 * (b + 1)],
                              0.0, mrun[:, :], op0=ALU.add, op1=ALU.max,
                              accum_out=mrun[:, :])))(),
                          (f"pss_F{b}", "mrun"), (sfn, "mrun"),
                          tag=f"sev{h}.{qt}.{b}")
                sc.op("vector",
                      (lambda i4=i4: (lambda e: e.tensor_scalar(
                          bias16[:, i4:i4 + 1], mrun[:, :], -cfg.scale, None,
                          op0=ALU.mult)))(),
                      ("mrun",), (f"bias16{i4}",), tag=f"b16.{h}.{qt}")
                sc.op("scalar",
                      (lambda span=span, pb=pb, i4=i4, sfb=sfb: (lambda e:
                          e.activation(
                              pb[:, 0:span], sfb[:, 0:span], AF.Exp,
                              bias=bias16[:, i4:i4 + 1], scale=cfg.scale,
                              accum_out=lsum[:, i4:i4 + 1])))(),
                      (sfn, f"bias16{i4}"), (f"pbuf{i4}", f"lsum{i4}"),
                      tag=f"ex{h}.{qt}")

            def s2a(h, qt):
                i4 = qt % 4
                tp = qt % 2
                pb = pbuf[:, i4 * S:i4 * S + S]
                nt = qt + 1
                def ptburst(pb=pb, nt=nt):
                    def f(e):
                        ins = None
                        for kt in range(nt):
                            ins = e.transpose(
                                pspt[:, 128 * kt:128 * (kt + 1)],
                                pb[:, 128 * kt:128 * (kt + 1)], id_sb[:, :])
                        return ins
                    return f
                sc.op("tensor", ptburst(), (f"pbuf{i4}", "id_sb"),
                      tuple(f"pst_F{u}" for u in range((nt + 3) // 4)),
                      tag=f"pt{h}.{qt}")
                sc.op("scalar",
                      (lambda nt=nt, tp=tp: (lambda e: e.activation(
                          ptb[:, tp * S:tp * S + 128 * nt],
                          pspt[:, 0:128 * nt], AF.Copy)))(),
                      tuple(f"pst_F{u}" for u in range((nt + 3) // 4)),
                      (f"ptb{tp}",), tag=f"pte{h}.{qt}")

            def s2b(h, qt):
                p = h % 2
                tp = qt % 2
                vp = h % 2
                def avm(qt=qt, tp=tp, vp=vp):
                    def f(e):
                        ins = None
                        for kt in range(qt + 1):
                            ins = e.matmul(
                                psav[:, 0:HD],
                                ptb[:, tp * S + 128 * kt:
                                    tp * S + 128 * (kt + 1)],
                                vtm[:, vp * TT * HD + HD * kt:
                                    vp * TT * HD + HD * (kt + 1)],
                                start=(kt == 0), stop=(kt == qt))
                        return ins
                    return f
                sc.op("tensor", avm(), (f"ptb{tp}", f"vtm{vp}"), ("psav_F",),
                      tag=f"av{h}.{qt}")

            def s2c_v(h, qt):
                i4 = qt % 4
                sc.op("vector",
                      (lambda i4=i4: (lambda e: e.reciprocal(
                          linv[:, i4:i4 + 1], lsum[:, i4:i4 + 1])))(),
                      (f"lsum{i4}",), (f"linv{i4}",), tag=f"rcp{h}.{qt}")
                sc.op("vector",
                      (lambda i4=i4: (lambda e: e.tensor_scalar(
                          avtm[:, :], psav[:, 0:HD], linv[:, i4:i4 + 1],
                          None, op0=ALU.mult)))(),
                      ("psav_F", f"linv{i4}"), ("avtm",), tag=f"ave{h}.{qt}")

            def s2c_pe(h, qt):
                p = h % 2
                def atr():
                    def f(e):
                        ins = None
                        for hcc in range(HC):
                            ins = e.transpose(
                                psvt[:, 128 * hcc:128 * (hcc + 1)],
                                avtm[:, 128 * hcc:128 * (hcc + 1)],
                                id_sb[:, :])
                        return ins
                    return f
                sc.op("tensor", atr(), ("avtm", "id_sb"), ("psvt_F",),
                      tag=f"atr{h}.{qt}")
                def aev(qt=qt, p=p):
                    def f(e):
                        return e.tensor_copy(
                            aoh[:, p * HS:(p + 1) * HS]
                            .rearrange("q (c s) -> q c s", c=HC)
                            [:, :, 128 * qt:128 * (qt + 1)],
                            psvt[:, 0:128 * HC]
                            .rearrange("q (c f) -> q c f", c=HC))
                    return f
                sc.op("vector", aev(), ("psvt_F",), (f"aoh{p}",),
                      tag=f"aev{h}.{qt}")

            flat = [(h, qt) for h in range(H) for qt in range(TT)]
            LAG_A, LAG_B, LAG_C = 2, 3, 4
            head_loads(0)
            head_vtrans(0)
            head_loads(1)
            N = len(flat)
            for fi in range(N + LAG_C):
                if fi >= LAG_C:
                    s2c_v(*flat[fi - LAG_C])
                if fi < N:
                    h, qt = flat[fi]
                    s1(h, qt)
                    if qt == LAG_B and h >= 1 and h + 1 < H:
                        head_loads(h + 1)
                    if qt == LAG_B + 2 and h + 1 < H:
                        head_vtrans(h + 1)
                if fi >= LAG_A and fi - LAG_A < N:
                    s2a(*flat[fi - LAG_A])
                if fi >= LAG_C:
                    s2c_pe(*flat[fi - LAG_C])
                if fi >= LAG_B and fi - LAG_B < N:
                    s2b(*flat[fi - LAG_B])
                if fi < N and flat[fi][1] == LAG_C and flat[fi][0] >= 1:
                    head_spill(flat[fi][0] - 1)
            head_spill(H - 1)
            sc.end_block()

        OPW = min(512, D)
        NOP = D // OPW
        with contextlib.ExitStack() as G:
            b2b_sb = G.enter_context(nc.sbuf_tensor("b2b_sb", [128, D], F32))
            wonat = G.enter_context(nc.sbuf_tensor("wonat", [128, 2 * D], BF16))
            woTq = G.enter_context(nc.sbuf_tensor("woTq", [128, DC * OPW], BF16))
            v2q = G.enter_context(nc.sbuf_tensor("v2q", [128, RC * OPW], BF16))
            aoutg = G.enter_context(nc.sbuf_tensor("aoutg", [128, 2 * DC * 512],
                                                   BF16))
            xres = G.enter_context(nc.sbuf_tensor("xres", [128, 2 * OPW], F32))
            outsb = G.enter_context(nc.sbuf_tensor("outsb", [128, 2 * OPW], F32))
            psf = G.enter_context(nc.psum_tensor("psf_G", [128, 1024], F32))
            pswt = G.enter_context(nc.psum_tensor("pswt_G", [128, 2048], BF16))

            sc.dma("sync", b2b_sb[:, :], b2b_d[:, :], (), ("b2b_sb",), "ld_b2b")
            wo_d = w_attn["wo"]

            def ao_load(op_i, tg):
                gp = tg % 2
                sc.dma("sync",
                       aoutg[:, gp * DC * 512:(gp + 1) * DC * 512]
                       .rearrange("q (n f) -> q n f", n=DC),
                       aout_d[tg, :, :, :].rearrange("n q f -> q n f"),
                       (f"aoutg{gp}",), (f"aoutg{gp}",), f"aold{op_i}.{tg}")

            for op_i in range(NOP):
                for slab in range(OPW // 128):
                    oc = (OPW * op_i) // 128 + slab
                    p = slab % 2
                    wo_ap = wonat[:, p * D:(p + 1) * D]
                    sc.dma("gpsimd", wo_ap, wo_d[128 * oc:128 * (oc + 1), :],
                           (f"wonat{p}",), (f"wonat{p}",), f"wold{op_i}.{slab}")
                    for g in range((DC + 3) // 4):
                        ics = list(range(4 * g, min(4 * g + 4, DC)))
                        sl = g % 2
                        def em(wo_ap=wo_ap, ics=ics, sl=sl):
                            def f(e):
                                ins = None
                                for kk, ic in enumerate(ics):
                                    ins = e.transpose(
                                        pswt[:, 1024 * sl + 128 * kk:
                                             1024 * sl + 128 * (kk + 1)],
                                        wo_ap[:, 128 * ic:128 * (ic + 1)],
                                        id_sb[:, :])
                                return ins
                            return f
                        sc.op("tensor", em(), (f"wonat{p}", "id_sb"),
                              (f"pswt_G{sl}",), tag=f"wotr{op_i}.{slab}.{g}")
                        def ev(ics=ics, sl=sl, slab=slab):
                            def f(e):
                                return e.tensor_copy(
                                    woTq[:, :].rearrange(
                                        "q (c o) -> q c o", c=DC)
                                    [:, ics[0]:ics[0] + len(ics),
                                     128 * slab:128 * (slab + 1)],
                                    pswt[:, 1024 * sl:
                                         1024 * sl + 128 * len(ics)]
                                    .rearrange("q (c f) -> q c f", c=len(ics)))
                            return f
                        sc.op("vector", ev(), (f"pswt_G{sl}",),
                              ("woTq",), tag=f"woev{op_i}.{slab}.{g}")
                sc.dma("gpsimd",
                       v2q[:, :].rearrange("q (n o) -> q n o", n=RC),
                       v2_d[:, OPW * op_i:OPW * (op_i + 1)]
                       .rearrange("(n q) o -> q n o", q=128),
                       ("v2q",), ("v2q",), f"v2ld{op_i}")

                if op_i == 0:
                    ao_load(0, 0)
                for tt in range(TT):
                    tg = tt // 4
                    gp = tg % 2
                    if tt % 4 == 1 and tg + 1 < TC:
                        ao_load(op_i, tg + 1)
                    p = tt % 2
                    xr_ap = xres[:, p * OPW:(p + 1) * OPW]
                    ou_ap = outsb[:, p * OPW:(p + 1) * OPW]
                    sc.dma("sync", xr_ap,
                           x_d[128 * tt:128 * (tt + 1),
                               OPW * op_i:OPW * (op_i + 1)],
                           (f"xres{p}",), (f"xres{p}",), f"xrl{op_i}.{tt}")
                    sc.op("gpsimd",
                          (lambda xr_ap=xr_ap, op_i=op_i: (lambda e:
                              e.tensor_tensor(
                                  xr_ap, xr_ap,
                                  b2b_sb[:, OPW * op_i:OPW * (op_i + 1)],
                                  ALU.add)))(),
                          (f"xres{p}", "b2b_sb"), (f"xres{p}",),
                          tag=f"xb2.{op_i}.{tt}")
                    def fmm(tt=tt, p=p, gp=gp):
                        def f(e):
                            ins = None
                            for i in range(DC):
                                ins = e.matmul(
                                    psf[:, 512 * p:512 * (p + 1)],
                                    aoutg[:, gp * DC * 512 + 512 * i
                                          + 128 * (tt % 4):
                                          gp * DC * 512 + 512 * i
                                          + 128 * (tt % 4 + 1)],
                                    woTq[:, OPW * i:OPW * i + 512],
                                    start=(i == 0), stop=False)
                            for r in range(RC):
                                ins = e.matmul(
                                    psf[:, 512 * p:512 * (p + 1)],
                                    zbuf[:, r * S + 128 * tt:
                                         r * S + 128 * (tt + 1)],
                                    v2q[:, OPW * r:OPW * r + 512],
                                    start=False, stop=(r == RC - 1))
                            return ins
                        return f
                    sc.op("tensor", fmm(), (f"aoutg{gp}", "woTq", "zbuf",
                                            "v2q"),
                          (f"psf_G{p}",), tag=f"fmm{op_i}.{tt}")
                    sc.op("vector",
                          (lambda xr_ap=xr_ap, ou_ap=ou_ap, p=p: (lambda e:
                              e.scalar_tensor_tensor(
                                  ou_ap, xr_ap, 1.0,
                                  psf[:, 512 * p:512 * (p + 1)],
                                  op0=ALU.mult, op1=ALU.add)))(),
                          (f"xres{p}", f"psf_G{p}"), (f"outsb{p}",),
                          tag=f"oev{op_i}.{tt}")
                    sc.dma("sync",
                           out_d[128 * tt:128 * (tt + 1),
                                 OPW * op_i:OPW * (op_i + 1)], ou_ap,
                           (f"outsb{p}",), ("OUT",), f"osp{op_i}.{tt}")
                if op_i + 1 < NOP:
                    ao_load(op_i + 1, 0)
            sc.end_block()
        st_z.close()

    return nc



def make_inmaps(cfg, roles, x_full):
    consts = make_consts(cfg)
    b1 = np.asarray(roles["b1"], dtype=np.float32)
    b2 = np.asarray(roles["b2"], dtype=np.float32)
    base = {k: np.ascontiguousarray(np.asarray(roles[k], dtype=np.float32))
            for k in ("wq", "wk", "wv", "wo", "u1", "v1", "u2", "v2")}
    base["b1pt"] = np.ascontiguousarray(b1.reshape(2 * cfg.IC, 128).T)
    base["b2b"] = np.ascontiguousarray(
        np.broadcast_to(b2, (128, cfg.D)).astype(np.float32))
    base.update({k: np.ascontiguousarray(v) for k, v in consts.items()})
    base["epsv"] = np.full((128, 1), cfg.eps, dtype=np.float32)
    return [dict(base, x=np.ascontiguousarray(x_full[b], dtype=np.float32))
            for b in range(x_full.shape[0])]


def kernel(L0_attn_hidden_states, L0_attn_wq, L0_attn_wk, L0_attn_wv,
           L0_attn_wo, L0_ffn_U1, L0_ffn_V1, L0_ffn_U2, L0_ffn_V2,
           L0_ffn_b1, L0_ffn_b2):
    cfg = CFG_FULL
    x = np.asarray(L0_attn_hidden_states, dtype=np.float32)
    B = x.shape[0]
    roles = {"wq": L0_attn_wq, "wk": L0_attn_wk, "wv": L0_attn_wv,
             "wo": L0_attn_wo, "u1": L0_ffn_U1, "v1": L0_ffn_V1,
             "u2": L0_ffn_U2, "v2": L0_ffn_V2, "b1": L0_ffn_b1,
             "b2": L0_ffn_b2}
    nc = build_nc(cfg)
    in_maps = make_inmaps(cfg, roles, x)
    res = run_bass_kernel_spmd(nc, in_maps, core_ids=list(range(B)))
    return np.stack([res.results[b]["out"] for b in range(B)], axis=0)


# version 14
# speedup vs baseline: 1.1002x; 1.1002x over previous
import math
import contextlib

import numpy as np
import ml_dtypes

import concourse.bass as bass
import concourse.mybir as mybir
from concourse.alu_op_type import AluOpType as ALU
from concourse.bass_utils import run_bass_kernel_spmd

F32 = mybir.dt.float32
BF16 = mybir.dt.bfloat16
AF = mybir.ActivationFunctionType
AX = mybir.AxisListType
bfdt = ml_dtypes.bfloat16

NEG = -1.0e9


class Cfg:
    def __init__(self, S=2048, D=4096, H=16, HD=256, ROT=64, RANK=1024,
                 INTER=11008, eps=1e-5):
        self.S, self.D, self.H, self.HD, self.ROT = S, D, H, HD, ROT
        self.RANK, self.INTER, self.eps = RANK, INTER, eps
        self.DC = D // 128
        self.TT = S // 128
        self.TC = S // 512
        self.RC = RANK // 128
        self.IC = INTER // 128
        self.HC = HD // 128
        self.scale = 1.0 / math.sqrt(HD)


CFG_FULL = Cfg()



def make_consts(cfg):
    S, ROT = cfg.S, cfg.ROT
    ident = np.eye(128, dtype=bfdt)
    rt = np.zeros((128, 128), dtype=np.float32)
    for i in range(ROT // 2):
        rt[2 * i + 1, 2 * i] = -1.0
        rt[2 * i, 2 * i + 1] = 1.0
    rt = rt.astype(bfdt)
    inv_freq = 1.0 / (10000.0 ** (np.arange(0, ROT, 2, dtype=np.float32) / ROT))
    ang = inv_freq[:, None] * np.arange(S, dtype=np.float32)[None, :]
    cosx = np.ones((128, S), dtype=np.float32)
    sinx = np.zeros((128, S), dtype=np.float32)
    for d in range(ROT):
        cosx[d, :] = np.cos(ang[d // 2, :])
        sinx[d, :] = np.sin(ang[d // 2, :])
    masks = np.full((128, 5 * 512), NEG, dtype=np.float32)
    for v in range(4):
        for p in range(128):
            masks[p, v * 512: v * 512 + min(512, 128 * v + p + 1)] = 0.0
    masks[:, 4 * 512:] = 0.0
    return {"ident": ident, "rt": rt, "cosx": cosx, "sinx": sinx,
            "masks": masks}



ENGS = ("sync", "gpsimd", "tensor", "vector", "scalar")
DMA_QUEUES = ("sync", "gpsimd")


class Rec:
    __slots__ = ("eng", "emit", "reads", "writes", "dma", "idx", "deps",
                 "need_inc", "inc_val", "tag", "drain_before", "waits",
                 "semkey")

    def __init__(self, eng, emit, reads, writes, dma, idx, tag):
        self.eng, self.emit = eng, emit
        self.reads, self.writes, self.dma = tuple(reads), tuple(writes), dma
        self.idx, self.tag = idx, tag
        self.deps = set()
        self.need_inc = dma
        self.inc_val = None
        self.semkey = None
        self.drain_before = False
        self.waits = []


NLANES = 8


class Sched:
    def __init__(self, nc):
        self.nc = nc
        self.recs = []
        self.writers = {}
        self.readers = {}
        self.sems = {}
        self.counters = {e + "_cmp": 0 for e in ENGS}
        for q in DMA_QUEUES:
            for l in range(NLANES):
                self.counters[f"{q}_dma{l}"] = 0
        self.dma_n = {q: 0 for q in DMA_QUEUES}
        self.nblocks = 0

    def op(self, eng, emit, reads=(), writes=(), dma=False, tag=""):
        r = Rec(eng, emit, reads, writes, dma, len(self.recs), tag)
        if dma:
            lane = self.dma_n[eng] % NLANES
            self.dma_n[eng] += 1
            r.semkey = f"{eng}_dma{lane}"
            self.counters[r.semkey] += 16
            r.inc_val = self.counters[r.semkey]
        else:
            r.semkey = eng + "_cmp"
        for b in r.reads:
            for w in self.writers.get(b, ()):
                r.deps.add(w)
        for b in r.writes:
            for w in self.writers.get(b, ()):
                r.deps.add(w)
            for rd in self.readers.get(b, ()):
                r.deps.add(rd)
        r.deps.discard(r)
        self.recs.append(r)
        for b in r.reads:
            self.readers.setdefault(b, []).append(r)
        for b in r.writes:
            self.writers.setdefault(b, []).append(r)
        return r

    def dma(self, eng, out_ap, in_ap, reads=(), writes=(), tag=""):
        self.op(eng, (lambda o=out_ap, i=in_ap: (lambda e: e.dma_start(o, i)))(),
                reads=reads, writes=writes, dma=True, tag=tag)

    def end_block(self):
        nc = self.nc
        recs = self.recs
        pos = {}
        stream_len = {e: 0 for e in ENGS}
        for r in recs:
            pos[r] = stream_len[r.eng]
            stream_len[r.eng] += 1
        drain_cover = {e: -1 for e in ENGS}
        for r in recs:
            for d in sorted(r.deps, key=lambda x: x.idx):
                if d.dma:
                    d.need_inc = True
                    r.waits.append(d)
                elif d.eng == r.eng:
                    if r.eng == "tensor":
                        continue
                    if pos[d] > drain_cover[r.eng]:
                        r.drain_before = True
                        drain_cover[r.eng] = pos[r] - 1
                else:
                    d.need_inc = True
                    r.waits.append(d)
        for r in recs:
            if r.need_inc and not r.dma:
                self.counters[r.semkey] += 1
                r.inc_val = self.counters[r.semkey]

        sems = self.sems
        counters = dict(self.counters)

        with nc.Block(f"blk{self.nblocks}") as blk:
            for eng_name in ENGS:
                myrecs = [r for r in recs if r.eng == eng_name]
                flushes = []
                if eng_name in DMA_QUEUES:
                    for l in range(NLANES):
                        k = f"{eng_name}_dma{l}"
                        if counters[k] > 0:
                            flushes.append((sems[k], counters[k]))
                if not myrecs and not flushes:
                    continue

                def make_section(myrecs=myrecs, flushes=flushes):
                    def section(e):
                        watermark = {}
                        for r in myrecs:
                            if r.drain_before:
                                e.drain()
                            best = {}
                            for d in r.waits:
                                if best.get(d.semkey, -1) < d.inc_val:
                                    best[d.semkey] = d.inc_val
                            if r.dma and r.inc_val > 16:
                                if best.get(r.semkey, -1) < r.inc_val - 16:
                                    best[r.semkey] = r.inc_val - 16
                            for k in sorted(best):
                                v = best[k]
                                if watermark.get(k, -1) >= v:
                                    continue
                                e.wait_ge(sems[k], v)
                                watermark[k] = v
                            instr = r.emit(e)
                            if r.need_inc:
                                if instr is None:
                                    raise RuntimeError("no instr: " + r.tag)
                                instr.then_inc(sems[r.semkey],
                                               16 if r.dma else 1)
                        for (s, v) in flushes:
                            e.wait_ge(s, v)
                    return section

                getattr(blk, eng_name)(make_section())

        self.recs = []
        self.writers = {}
        self.readers = {}
        self.nblocks += 1



def build_nc(cfg, debug=False):
    nc = bass.Bass("TRN2", target_bir_lowering=False)
    S, D, H, HD, ROT, RANK, INTER = (cfg.S, cfg.D, cfg.H, cfg.HD, cfg.ROT,
                                     cfg.RANK, cfg.INTER)
    DC, TT, TC, RC, IC, HC = cfg.DC, cfg.TT, cfg.TC, cfg.RC, cfg.IC, cfg.HC

    x_d = nc.dram_tensor("x", [S, D], F32, kind="ExternalInput")
    w_attn = {n: nc.dram_tensor(n, [D, D], F32, kind="ExternalInput")
              for n in ("wq", "wk", "wv", "wo")}
    u1_d = nc.dram_tensor("u1", [D, RANK], F32, kind="ExternalInput")
    v1_d = nc.dram_tensor("v1", [RANK, 2 * INTER], F32, kind="ExternalInput")
    u2_d = nc.dram_tensor("u2", [INTER, RANK], F32, kind="ExternalInput")
    v2_d = nc.dram_tensor("v2", [RANK, D], F32, kind="ExternalInput")
    b1pt_d = nc.dram_tensor("b1pt", [128, 2 * IC], F32, kind="ExternalInput")
    b2b_d = nc.dram_tensor("b2b", [128, D], F32, kind="ExternalInput")
    ident_d = nc.dram_tensor("ident", [128, 128], BF16, kind="ExternalInput")
    rt_d = nc.dram_tensor("rt", [128, 128], BF16, kind="ExternalInput")
    cosx_d = nc.dram_tensor("cosx", [128, S], F32, kind="ExternalInput")
    sinx_d = nc.dram_tensor("sinx", [128, S], F32, kind="ExternalInput")
    masks_d = nc.dram_tensor("masks", [128, 5 * 512], F32, kind="ExternalInput")
    epsv_d = nc.dram_tensor("epsv", [128, 1], F32, kind="ExternalInput")

    out_d = nc.dram_tensor("out", [S, D], F32, kind="ExternalOutput")
    dk = dict(kind="ExternalOutput") if debug else {}
    qf_d = nc.dram_tensor("qf", [D, S], BF16, **dk)
    kf_d = nc.dram_tensor("kf", [D, S], BF16, **dk)
    vf_d = nc.dram_tensor("vf", [D, S], BF16, **dk)
    act_d = nc.dram_tensor("actsp", [IC * 128, S], BF16, **dk)
    u2bf_d = nc.dram_tensor("u2bf", [INTER, RANK], BF16)
    aout_d = nc.dram_tensor("aoutsp", [TC, DC, 128, 512], BF16, **dk)

    sc = Sched(nc)

    with contextlib.ExitStack() as ctx:
        semkeys = [e + "_cmp" for e in ENGS]
        for q in DMA_QUEUES:
            semkeys += [f"{q}_dma{l}" for l in range(NLANES)]
        for key in semkeys:
            sc.sems[key] = ctx.enter_context(nc.semaphore(key))

        id_sb = ctx.enter_context(nc.sbuf_tensor("id_sb", [128, 128], BF16))
        eps_sb = ctx.enter_context(nc.sbuf_tensor("eps_sb", [128, 1], F32))
        rt_sb = ctx.enter_context(nc.sbuf_tensor("rt_sb", [128, 128], BF16))
        b1_sb = ctx.enter_context(nc.sbuf_tensor("b1_sb", [128, 2 * IC], F32))

        st_xk = contextlib.ExitStack()
        st_cs = contextlib.ExitStack()
        st_psl = contextlib.ExitStack()
        st_z = contextlib.ExitStack()

        xk = st_xk.enter_context(nc.sbuf_tensor("xk", [128, DC * S], BF16))
        cos_sb = st_cs.enter_context(nc.sbuf_tensor("cos_sb", [128, S], F32))
        sin_sb = st_cs.enter_context(nc.sbuf_tensor("sin_sb", [128, S], F32))

        with contextlib.ExitStack() as A:
            x_in = A.enter_context(nc.sbuf_tensor("x_in", [128, 2 * D], F32))
            xn_tm = A.enter_context(nc.sbuf_tensor("xn_tm", [128, 2 * D], BF16))
            nstat = D // 512
            stats = A.enter_context(nc.sbuf_tensor("stats", [128, nstat * 6], F32))
            mv = A.enter_context(nc.sbuf_tensor("mv", [128, 2], F32))
            rstd = A.enter_context(nc.sbuf_tensor("rstd", [128, 1], F32))
            nm = A.enter_context(nc.sbuf_tensor("nm", [128, 1], F32))
            psT = A.enter_context(nc.psum_tensor("psT_A", [128, 2048], BF16))

            sc.dma("sync", id_sb[:, :], ident_d[:, :], (), ("id_sb",), "ld_id")
            sc.dma("sync", eps_sb[:, :], epsv_d[:, :], (), ("eps_sb",), "ld_eps")
            sc.dma("sync", rt_sb[:, :], rt_d[:, :], (), ("rt_sb",), "ld_rt")
            sc.dma("sync", cos_sb[:, :], cosx_d[:, :], (), ("cos_sb",), "ld_cos")
            sc.dma("sync", sin_sb[:, :], sinx_d[:, :], (), ("sin_sb",), "ld_sin")
            sc.dma("sync", b1_sb[:, :], b1pt_d[:, :], (), ("b1_sb",), "ld_b1")

            for tt in range(TT):
                p = tt % 2
                xi = x_in[:, p * D:(p + 1) * D]
                xn = xn_tm[:, p * D:(p + 1) * D]
                sc.dma("sync", xi, x_d[128 * tt:128 * (tt + 1), :],
                       (f"x_in{p}",), (f"x_in{p}",), f"ldx{tt}")
                for j in range(nstat):
                    sc.op("vector",
                          (lambda xi=xi, j=j: (lambda e: e.bn_stats(
                              stats[:, 6 * j:6 * (j + 1)],
                              xi[:, 512 * j:512 * (j + 1)])))(),
                          (f"x_in{p}",), ("stats",), tag=f"bnst{tt}.{j}")
                sc.op("vector",
                      (lambda: (lambda e: e.bn_aggr(mv[:, :], stats[:, :])))(),
                      ("stats",), ("mv",), tag=f"bnag{tt}")
                sc.op("scalar",
                      (lambda: (lambda e: e.activation(
                          rstd[:, :], mv[:, 1:2], AF.Ln,
                          bias=eps_sb[:, :])))(),
                      ("mv", "eps_sb"), ("rstd",), tag=f"lnv{tt}")
                sc.op("scalar",
                      (lambda: (lambda e: e.activation(
                          rstd[:, :], rstd[:, :], AF.Exp, scale=-0.5)))(),
                      ("rstd",), ("rstd",), tag=f"expv{tt}")
                sc.op("vector",
                      (lambda: (lambda e: e.tensor_tensor(
                          nm[:, :], mv[:, 0:1], rstd[:, :], ALU.mult)))(),
                      ("mv", "rstd"), ("nm",), tag=f"nm{tt}")
                sc.op("vector",
                      (lambda xi=xi, xn=xn: (lambda e: e.tensor_scalar(
                          xn, xi, rstd[:, :], nm[:, :],
                          op0=ALU.mult, op1=ALU.subtract)))(),
                      (f"x_in{p}", "rstd", "nm"), (f"xn_tm{p}",), tag=f"nrm{tt}")
                for g in range((DC + 3) // 4):
                    ics = list(range(4 * g, min(4 * g + 4, DC)))
                    sl = g % 2
                    def em(xn=xn, ics=ics, sl=sl):
                        def f(e):
                            ins = None
                            for k, ic in enumerate(ics):
                                ins = e.transpose(
                                    psT[:, 1024 * sl + 128 * k:
                                        1024 * sl + 128 * (k + 1)],
                                    xn[:, 128 * ic:128 * (ic + 1)], id_sb[:, :])
                            return ins
                        return f
                    sc.op("tensor", em(), (f"xn_tm{p}", "id_sb"), (f"psT_A{sl}",),
                          tag=f"xtr{tt}.{g}")
                    def ev(tt=tt, ics=ics, sl=sl):
                        def f(e):
                            return e.tensor_copy(
                                xk[:, :].rearrange("q (c s) -> q c s", c=DC)
                                [:, ics[0]:ics[0] + len(ics),
                                 128 * tt:128 * (tt + 1)],
                                psT[:, 1024 * sl:1024 * sl + 128 * len(ics)]
                                .rearrange("q (c s) -> q c s", c=len(ics)))
                        return f
                    sc.op("vector", ev(), (f"psT_A{sl}",), ("xk",),
                          tag=f"xev{tt}.{g}")
            sc.end_block()

        with contextlib.ExitStack() as B:
            wnat = B.enter_context(nc.sbuf_tensor("wnat", [128, 2 * D], BF16))
            wT = B.enter_context(nc.sbuf_tensor("wT", [128, 2 * D], BF16))
            qsb = B.enter_context(nc.sbuf_tensor("qsb", [128, 2 * S], BF16))
            qout = B.enter_context(nc.sbuf_tensor("qout", [128, 2 * S], BF16))
            trope = B.enter_context(nc.sbuf_tensor("trope", [128, 1024], F32))
            psa = B.enter_context(nc.psum_tensor("psa_B", [128, S], F32))
            psb = B.enter_context(nc.psum_tensor("psb_B", [128, 1024], F32))
            psTB = B.enter_context(nc.psum_tensor("psT_B", [128, 2048], BF16))

            half = IC * 128 // 2
            sc.dma("gpsimd", u2bf_d[0:half, :], u2_d[0:half, :],
                   (), ("U2BF",), "u2cast0")
            sc.dma("gpsimd", u2bf_d[half:, :], u2_d[half:, :],
                   (), ("U2BF",), "u2cast1")

            WLIST = [("wq", qf_d, 0), ("wk", kf_d, 1), ("wv", vf_d, 2)]
            jobs = [(wi, oc) for wi in range(3) for oc in range(DC)]

            def wload(j):
                wi, oc = jobs[j]
                p = j % 2
                sc.dma("gpsimd", wnat[:, p * D:(p + 1) * D],
                       w_attn[WLIST[wi][0]][128 * oc:128 * (oc + 1), :],
                       (f"wnat{p}",), (f"wnat{p}",), f"wld{wi}.{oc}")

            def wtrans(j):
                wi, oc = jobs[j]
                p = j % 2
                wn_ap = wnat[:, p * D:(p + 1) * D]
                wT_ap = wT[:, p * D:(p + 1) * D]
                for g in range((DC + 3) // 4):
                    ics = list(range(4 * g, min(4 * g + 4, DC)))
                    sl = g % 2
                    def em(wn_ap=wn_ap, ics=ics, sl=sl):
                        def f(e):
                            ins = None
                            for kk, ic in enumerate(ics):
                                ins = e.transpose(
                                    psTB[:, 1024 * sl + 128 * kk:
                                         1024 * sl + 128 * (kk + 1)],
                                    wn_ap[:, 128 * ic:128 * (ic + 1)],
                                    id_sb[:, :])
                            return ins
                        return f
                    sc.op("tensor", em(), (f"wnat{p}", "id_sb"),
                          (f"psT_B{sl}",), tag=f"wtr{wi}.{oc}.{g}")
                    def ev(wT_ap=wT_ap, ics=ics, sl=sl):
                        def f(e):
                            return e.tensor_copy(
                                wT_ap[:, 128 * ics[0]:
                                      128 * (ics[0] + len(ics))],
                                psTB[:, 1024 * sl:1024 * sl + 128 * len(ics)])
                        return f
                    sc.op("vector", ev(), (f"psT_B{sl}",), (f"wT{p}",),
                          tag=f"wev{wi}.{oc}.{g}")

            rope_pend = []
            rp_ctr = [0]

            def emit_rope(wi, oc, rp, p):
                qo_ap = qout[:, p * S:(p + 1) * S]
                qs_ap = qsb[:, rp * S:rp * S + S]
                for tc in range(TC):
                    bk = tc % 2
                    sc.op("tensor",
                          (lambda tc=tc, bk=bk, qs_ap=qs_ap: (lambda e:
                              e.matmul(
                                  psb[:, 512 * bk:512 * (bk + 1)],
                                  rt_sb[:, :],
                                  qs_ap[:, 512 * tc:512 * (tc + 1)],
                                  start=True, stop=True)))(),
                          ("rt_sb", f"qsb{rp}"), (f"psb_B{bk}",),
                          tag=f"rq{wi}.{oc}.{tc}")
                    sc.op("vector",
                          (lambda tc=tc, bk=bk, qs_ap=qs_ap: (lambda e:
                              e.tensor_tensor(
                                  trope[:, 512 * bk:512 * (bk + 1)],
                                  qs_ap[:, 512 * tc:512 * (tc + 1)],
                                  cos_sb[:, 512 * tc:512 * (tc + 1)],
                                  ALU.mult)))(),
                          (f"qsb{rp}", "cos_sb"), (f"trope{bk}",),
                          tag=f"rc1.{wi}.{oc}.{tc}")
                    sc.op("vector",
                          (lambda tc=tc, bk=bk, qo_ap=qo_ap: (lambda e:
                              e.scalar_tensor_tensor(
                                  qo_ap[:, 512 * tc:512 * (tc + 1)],
                                  psb[:, 512 * bk:512 * (bk + 1)], 1.0,
                                  sin_sb[:, 512 * tc:512 * (tc + 1)],
                                  op0=ALU.mult, op1=ALU.mult)))(),
                          (f"psb_B{bk}", "sin_sb"), (f"qout{p}",),
                          tag=f"rc2.{wi}.{oc}.{tc}")
                    sc.op("vector",
                          (lambda tc=tc, bk=bk, qo_ap=qo_ap: (lambda e:
                              e.tensor_tensor(
                                  qo_ap[:, 512 * tc:512 * (tc + 1)],
                                  qo_ap[:, 512 * tc:512 * (tc + 1)],
                                  trope[:, 512 * bk:512 * (bk + 1)],
                                  ALU.add)))(),
                          (f"trope{bk}", f"qout{p}"), (f"qout{p}",),
                          tag=f"rc3.{wi}.{oc}.{tc}")
                sc.dma("sync",
                       WLIST[wi][1][128 * oc:128 * (oc + 1), :], qo_ap,
                       (f"qout{p}",), (f"OF{wi}",), f"qsp{wi}.{oc}")

            wload(0)
            wload(1)
            wtrans(0)
            for j, (wi, oc) in enumerate(jobs):
                p = j % 2
                wT_ap = wT[:, p * D:(p + 1) * D]
                qo_ap = qout[:, p * S:(p + 1) * S]
                rope = (wi < 2) and (oc % HC == 0) and ROT > 0
                if j + 2 < len(jobs):
                    wload(j + 2)
                if j + 1 < len(jobs):
                    wtrans(j + 1)
                while rope_pend:
                    emit_rope(*rope_pend.pop(0))
                for tc in range(TC):
                    def mgq(wT_ap=wT_ap, tc=tc):
                        def f(e):
                            ins = None
                            for i in range(DC):
                                ins = e.matmul(
                                    psa[:, 512 * tc:512 * (tc + 1)],
                                    wT_ap[:, 128 * i:128 * (i + 1)],
                                    xk[:, i * S + 512 * tc:
                                       i * S + 512 * (tc + 1)],
                                    start=(i == 0), stop=(i == DC - 1))
                            return ins
                        return f
                    sc.op("tensor", mgq(), (f"wT{p}", "xk"),
                          (f"psa_B{tc}",), tag=f"mm{wi}.{oc}.{tc}")
                if not rope:
                    for tc in range(TC):
                        sc.op("vector",
                              (lambda qo_ap=qo_ap, tc=tc: (lambda e:
                                  e.tensor_copy(
                                      qo_ap[:, 512 * tc:512 * (tc + 1)],
                                      psa[:, 512 * tc:512 * (tc + 1)])))(),
                              (f"psa_B{tc}",), (f"qout{p}",),
                              tag=f"qev{wi}.{oc}.{tc}")
                    sc.dma("sync",
                           WLIST[wi][1][128 * oc:128 * (oc + 1), :], qo_ap,
                           (f"qout{p}",), (f"OF{wi}",), f"qsp{wi}.{oc}")
                else:
                    rp = rp_ctr[0] % 2
                    rp_ctr[0] += 1
                    qs_ap = qsb[:, rp * S:rp * S + S]
                    for tc in range(TC):
                        sc.op("vector",
                              (lambda qs_ap=qs_ap, tc=tc: (lambda e:
                                  e.tensor_copy(
                                      qs_ap[:, 512 * tc:512 * (tc + 1)],
                                      psa[:, 512 * tc:512 * (tc + 1)])))(),
                              (f"psa_B{tc}",), (f"qsb{rp}",),
                              tag=f"qevr{wi}.{oc}.{tc}")
                    rope_pend.append((wi, oc, rp, p))
            while rope_pend:
                emit_rope(*rope_pend.pop(0))
            sc.end_block()
        st_cs.close()

        psl = st_psl.enter_context(nc.sbuf_tensor("psl", [128, RC * S], BF16))
        with contextlib.ExitStack() as C:
            u1sl = C.enter_context(nc.sbuf_tensor("u1sl", [128, 2 * DC * 128],
                                                  BF16))
            psaC = C.enter_context(nc.psum_tensor("psa_C", [128, S], F32))
            for rc in range(RC):
                p = rc % 2
                u1_ap = u1sl[:, p * DC * 128:(p + 1) * DC * 128]
                sc.dma("gpsimd",
                       u1_ap.rearrange("q (n r) -> q n r", n=DC),
                       u1_d[:, 128 * rc:128 * (rc + 1)]
                       .rearrange("(n q) r -> q n r", q=128),
                       (f"u1sl{p}",), (f"u1sl{p}",), f"u1ld{rc}")
                for tc in range(TC):
                    def mgq(u1_ap=u1_ap, tc=tc):
                        def f(e):
                            ins = None
                            for i in range(DC):
                                ins = e.matmul(
                                    psaC[:, 512 * tc:512 * (tc + 1)],
                                    u1_ap[:, 128 * i:128 * (i + 1)],
                                    xk[:, i * S + 512 * tc:
                                       i * S + 512 * (tc + 1)],
                                    start=(i == 0), stop=(i == DC - 1))
                            return ins
                        return f
                    sc.op("tensor", mgq(), (f"u1sl{p}", "xk"),
                          (f"psa_C{tc}",), tag=f"pmm{rc}.{tc}")
                    sc.op("vector",
                          (lambda rc=rc, tc=tc: (lambda e: e.tensor_copy(
                              psl[:, rc * S + 512 * tc:
                                  rc * S + 512 * (tc + 1)],
                              psaC[:, 512 * tc:512 * (tc + 1)])))(),
                          (f"psa_C{tc}",), ("psl",), tag=f"pev{rc}.{tc}")
            sc.end_block()

        with contextlib.ExitStack() as Dk:
            v1g = Dk.enter_context(nc.sbuf_tensor("v1g", [128, 2 * RC * 128], BF16))
            v1u = Dk.enter_context(nc.sbuf_tensor("v1u", [128, 2 * RC * 128], BF16))
            silu_f = Dk.enter_context(nc.sbuf_tensor("silu_f", [128, S], F32))
            up_f = Dk.enter_context(nc.sbuf_tensor("up_f", [128, S], F32))
            actst = Dk.enter_context(nc.sbuf_tensor("actst", [128, 2 * S], BF16))
            psg = Dk.enter_context(nc.psum_tensor("psg_D", [128, S], F32))
            psu = Dk.enter_context(nc.psum_tensor("psu_D", [128, S], F32))
            W = RC * 128
            for c in range(IC):
                p = c % 2
                g_ap = v1g[:, p * W:(p + 1) * W]
                u_ap = v1u[:, p * W:(p + 1) * W]
                a_ap = actst[:, p * S:(p + 1) * S]
                sc.dma("gpsimd", g_ap.rearrange("q (n f) -> q n f", n=RC),
                       v1_d[:, 128 * c:128 * (c + 1)]
                       .rearrange("(n q) f -> q n f", q=128),
                       (f"v1g{p}",), (f"v1g{p}",), f"v1gld{c}")
                sc.dma("gpsimd", u_ap.rearrange("q (n f) -> q n f", n=RC),
                       v1_d[:, 128 * (IC + c):128 * (IC + c + 1)]
                       .rearrange("(n q) f -> q n f", q=128),
                       (f"v1u{p}",), (f"v1u{p}",), f"v1uld{c}")
                for which, w_ap, ps, nm_ in (("g", g_ap, psg, "psg_D"),
                                             ("u", u_ap, psu, "psu_D")):
                    def mg(w_ap=w_ap, ps=ps):
                        def f(e):
                            ins = None
                            for r in range(RC):
                                for tc in range(TC):
                                    ins = e.matmul(
                                        ps[:, 512 * tc:512 * (tc + 1)],
                                        w_ap[:, 128 * r:128 * (r + 1)],
                                        psl[:, r * S + 512 * tc:
                                            r * S + 512 * (tc + 1)],
                                        start=(r == 0), stop=(r == RC - 1))
                            return ins
                        return f
                    sc.op("tensor", mg(),
                          (f"v1{which}{p}", "psl"), (nm_,), tag=f"h{which}{c}")
                sc.op("scalar",
                      (lambda c=c: (lambda e: e.activation(
                          silu_f[:, :], psg[:, :], AF.Silu,
                          bias=b1_sb[:, c:c + 1])))(),
                      ("psg_D", "b1_sb"), ("silu_f",), tag=f"sil{c}")
                sc.op("scalar",
                      (lambda c=c: (lambda e: e.activation(
                          up_f[:, :], psu[:, :], AF.Identity,
                          bias=b1_sb[:, IC + c:IC + c + 1])))(),
                      ("psu_D", "b1_sb"), ("up_f",), tag=f"up{c}")
                sc.op("vector",
                      (lambda a_ap=a_ap: (lambda e: e.tensor_tensor(
                          a_ap, silu_f[:, :], up_f[:, :], ALU.mult)))(),
                      ("silu_f", "up_f"), (f"actst{p}",), tag=f"am{c}")
                sc.dma("sync", act_d[128 * c:128 * (c + 1), :], a_ap,
                       (f"actst{p}",), ("ACT",), f"asp{c}")
            sc.end_block()
        st_psl.close()
        st_xk.close()

        zbuf = st_z.enter_context(nc.sbuf_tensor("zbuf", [128, RC * S], BF16))
        with contextlib.ExitStack() as E:
            u2row = E.enter_context(nc.sbuf_tensor("u2row", [128, 3 * RANK], BF16))
            acti = E.enter_context(nc.sbuf_tensor("acti", [128, 3 * 512], BF16))
            psz = E.enter_context(nc.psum_tensor("psz_E", [128, RC * 512], F32))
            for tc in range(TC):
                for ic in range(IC):
                    p = ic % 3
                    u2_ap = u2row[:, p * RANK:(p + 1) * RANK]
                    ac_ap = acti[:, p * 512:(p + 1) * 512]
                    sc.dma("sync", u2_ap, u2bf_d[128 * ic:128 * (ic + 1), :],
                           (f"u2row{p}",), (f"u2row{p}",), f"u2ld{tc}.{ic}")
                    sc.dma("gpsimd", ac_ap,
                           act_d[128 * ic:128 * (ic + 1),
                                 512 * tc:512 * (tc + 1)],
                           (f"acti{p}",), (f"acti{p}",), f"ald{tc}.{ic}")
                    def mg(u2_ap=u2_ap, ac_ap=ac_ap, ic=ic):
                        def f(e):
                            ins = None
                            for r in range(RC):
                                ins = e.matmul(
                                    psz[:, 512 * r:512 * (r + 1)],
                                    u2_ap[:, 128 * r:128 * (r + 1)], ac_ap,
                                    start=(ic == 0), stop=(ic == IC - 1))
                            return ins
                        return f
                    sc.op("tensor", mg(), (f"u2row{p}", f"acti{p}"), ("psz_E",),
                          tag=f"zmm{tc}.{ic}")
                sc.op("vector",
                      (lambda tc=tc: (lambda e: e.tensor_copy(
                          zbuf[:, :].rearrange("q (r s) -> q r s", r=RC)
                          [:, :, 512 * tc:512 * (tc + 1)],
                          psz[:, :].rearrange("q (r f) -> q r f", r=RC))))(),
                      ("psz_E",), ("zbuf",), tag=f"zev{tc}")
            sc.end_block()

        with contextlib.ExitStack() as Fk:
            mask_sb = Fk.enter_context(nc.sbuf_tensor("mask_sb", [128, 5 * 512], F32))
            qh = Fk.enter_context(nc.sbuf_tensor("qh", [128, 2 * HC * S], BF16))
            kh = Fk.enter_context(nc.sbuf_tensor("kh", [128, 2 * HC * S], BF16))
            vhf = Fk.enter_context(nc.sbuf_tensor("vhf", [128, 2 * HC * S], BF16))
            vtm = Fk.enter_context(nc.sbuf_tensor("vtm", [128, 2 * TT * HD], BF16))
            aoh = Fk.enter_context(nc.sbuf_tensor("aoh", [128, 2 * HC * S], BF16))
            pbuf = Fk.enter_context(nc.sbuf_tensor("pbuf", [128, 4 * S], BF16))
            SW = 512 * ((TT - 1) // 4 + 1)
            sfb_t = Fk.enter_context(nc.sbuf_tensor("sfb", [128, 2 * SW], F32))
            ptb = Fk.enter_context(nc.sbuf_tensor("ptb", [128, 2 * S], BF16))
            avtm = Fk.enter_context(nc.sbuf_tensor("avtm", [128, HD], BF16))
            mrun = Fk.enter_context(nc.sbuf_tensor("mrun", [128, 1], F32))
            bias16 = Fk.enter_context(nc.sbuf_tensor("bias16", [128, 4], F32))
            lsum = Fk.enter_context(nc.sbuf_tensor("lsum", [128, 4], F32))
            linv = Fk.enter_context(nc.sbuf_tensor("linv", [128, 4], F32))
            pss = Fk.enter_context(nc.psum_tensor("pss_F", [128, 512 * ((TT - 1) // 4 + 1)], F32))
            pspt = Fk.enter_context(nc.psum_tensor("pst_F", [128, 2048], BF16))
            psav = Fk.enter_context(nc.psum_tensor("psav_F", [128, 512], F32))
            psvt = Fk.enter_context(nc.psum_tensor("psvt_F", [128, 512], BF16))

            sc.dma("sync", mask_sb[:, :], masks_d[:, :], (), ("mask_sb",),
                   "ld_mask")
            HS = HC * S

            def head_loads(h):
                p = h % 2
                for nmq, off, src_d, bufn in (
                        (f"qh{p}", 0, qf_d, "OF0"),
                        (f"kh{p}", 0, kf_d, "OF1"),
                        (f"vhf{p}", 0, vf_d, "OF2")):
                    ap = {f"qh{p}": qh, f"kh{p}": kh, f"vhf{p}": vhf}[nmq]
                    sc.dma("sync",
                           ap[:, p * HS:(p + 1) * HS]
                           .rearrange("q (c s) -> q c s", c=HC),
                           src_d[HD * h:HD * (h + 1), :]
                           .rearrange("(c q) s -> q c s", q=128),
                           (nmq,), (nmq,), f"hld.{nmq}.{h}")

            def head_vtrans(h):
                p = h % 2
                vp = h % 2
                v_ap = vhf[:, p * HS:(p + 1) * HS]
                pairs = [(tt, hcc) for tt in range(TT) for hcc in range(HC)]
                for g in range(0, len(pairs), 4):
                    grp = pairs[g:g + 4]
                    sl = (g // 4) % 2
                    def em(v_ap=v_ap, grp=grp, sl=sl):
                        def f(e):
                            ins = None
                            for k, (tt, hcc) in enumerate(grp):
                                ins = e.transpose(
                                    pspt[:, 1024 * sl + 128 * k:
                                         1024 * sl + 128 * (k + 1)],
                                    v_ap[:, hcc * S + 128 * tt:
                                         hcc * S + 128 * (tt + 1)],
                                    id_sb[:, :])
                            return ins
                        return f
                    sc.op("tensor", em(), (f"vhf{p}", "id_sb"),
                          (f"pst_F{2 * sl}", f"pst_F{2 * sl + 1}"),
                          tag=f"vtr{h}.{g}")
                    def ev(grp=grp, sl=sl, vp=vp):
                        def f(e):
                            base = vp * TT * HD + (grp[0][0] * HC + grp[0][1]) * 128
                            return e.tensor_copy(
                                vtm[:, base:base + 128 * len(grp)],
                                pspt[:, 1024 * sl:1024 * sl + 128 * len(grp)])
                        return f
                    sc.op("vector", ev(),
                          (f"pst_F{2 * sl}", f"pst_F{2 * sl + 1}"),
                          (f"vtm{vp}",), tag=f"vev{h}.{g}")

            def head_spill(h):
                p = h % 2
                for tg in range(TC):
                    for hcc in range(HC):
                        sc.dma("sync", aout_d[tg, h * HC + hcc, :, :],
                               aoh[:, p * HS + hcc * S + 512 * tg:
                                   p * HS + hcc * S + 512 * (tg + 1)],
                               (f"aoh{p}",), ("AOUT",), f"aosp{h}.{tg}.{hcc}")

            def s1(h, qt):
                p = h % 2
                i4 = qt % 4
                sp = qt % 2
                q_ap = qh[:, p * HS:(p + 1) * HS]
                k_ap = kh[:, p * HS:(p + 1) * HS]
                nkb = qt // 4 + 1
                span = 512 * nkb
                pb = pbuf[:, i4 * S:i4 * S + S]
                sfb = sfb_t[:, sp * SW:sp * SW + SW]
                sfn = f"sfb{sp}"
                def smm(q_ap=q_ap, k_ap=k_ap, qt=qt, nkb=nkb):
                    def f(e):
                        ins = None
                        for kb in range(nkb):
                            for hcc in range(HC):
                                ins = e.matmul(
                                    pss[:, 512 * kb:512 * (kb + 1)],
                                    q_ap[:, hcc * S + 128 * qt:
                                         hcc * S + 128 * (qt + 1)],
                                    k_ap[:, hcc * S + 512 * kb:
                                         hcc * S + 512 * (kb + 1)],
                                    start=(hcc == 0), stop=(hcc == HC - 1))
                        return ins
                    return f
                sc.op("tensor", smm(), (f"qh{p}", f"kh{p}"),
                      tuple(f"pss_F{b}" for b in range(nkb)),
                      tag=f"s{h}.{qt}")
                db = nkb - 1
                sc.op("vector",
                      (lambda db=db, qt=qt, sfb=sfb: (lambda e:
                          e.scalar_tensor_tensor(
                              sfb[:, 512 * db:512 * (db + 1)],
                              pss[:, 512 * db:512 * (db + 1)], 1.0,
                              mask_sb[:, 512 * (qt % 4):512 * (qt % 4 + 1)],
                              op0=ALU.mult, op1=ALU.add)))(),
                      (f"pss_F{db}", "mask_sb"), (sfn,), tag=f"smk{h}.{qt}")
                sc.op("vector",
                      (lambda db=db, sfb=sfb: (lambda e: e.tensor_scalar(
                          sfb[:, 512 * db:512 * (db + 1)],
                          sfb[:, 512 * db:512 * (db + 1)],
                          0.0, None, op0=ALU.add, op1=ALU.max,
                          accum_out=mrun[:, :])))(),
                      (sfn,), (sfn, "mrun"), tag=f"sm0{h}.{qt}")
                for b in range(nkb - 1):
                    sc.op("vector",
                          (lambda b=b, sfb=sfb: (lambda e: e.tensor_scalar(
                              sfb[:, 512 * b:512 * (b + 1)],
                              pss[:, 512 * b:512 * (b + 1)],
                              0.0, mrun[:, :], op0=ALU.add, op1=ALU.max,
                              accum_out=mrun[:, :])))(),
                          (f"pss_F{b}", "mrun"), (sfn, "mrun"),
                          tag=f"sev{h}.{qt}.{b}")
                sc.op("vector",
                      (lambda i4=i4: (lambda e: e.tensor_scalar(
                          bias16[:, i4:i4 + 1], mrun[:, :], -cfg.scale, None,
                          op0=ALU.mult)))(),
                      ("mrun",), (f"bias16{i4}",), tag=f"b16.{h}.{qt}")
                sc.op("scalar",
                      (lambda span=span, pb=pb, i4=i4, sfb=sfb: (lambda e:
                          e.activation(
                              pb[:, 0:span], sfb[:, 0:span], AF.Exp,
                              bias=bias16[:, i4:i4 + 1], scale=cfg.scale,
                              accum_out=lsum[:, i4:i4 + 1])))(),
                      (sfn, f"bias16{i4}"), (f"pbuf{i4}", f"lsum{i4}"),
                      tag=f"ex{h}.{qt}")

            def s2a(h, qt):
                i4 = qt % 4
                tp = qt % 2
                pb = pbuf[:, i4 * S:i4 * S + S]
                nt = qt + 1
                def ptburst(pb=pb, nt=nt):
                    def f(e):
                        ins = None
                        for kt in range(nt):
                            ins = e.transpose(
                                pspt[:, 128 * kt:128 * (kt + 1)],
                                pb[:, 128 * kt:128 * (kt + 1)], id_sb[:, :])
                        return ins
                    return f
                sc.op("tensor", ptburst(), (f"pbuf{i4}", "id_sb"),
                      tuple(f"pst_F{u}" for u in range((nt + 3) // 4)),
                      tag=f"pt{h}.{qt}")
                sc.op("scalar",
                      (lambda nt=nt, tp=tp: (lambda e: e.activation(
                          ptb[:, tp * S:tp * S + 128 * nt],
                          pspt[:, 0:128 * nt], AF.Copy)))(),
                      tuple(f"pst_F{u}" for u in range((nt + 3) // 4)),
                      (f"ptb{tp}",), tag=f"pte{h}.{qt}")

            def s2b(h, qt):
                p = h % 2
                tp = qt % 2
                vp = h % 2
                def avm(qt=qt, tp=tp, vp=vp):
                    def f(e):
                        ins = None
                        for kt in range(qt + 1):
                            ins = e.matmul(
                                psav[:, 0:HD],
                                ptb[:, tp * S + 128 * kt:
                                    tp * S + 128 * (kt + 1)],
                                vtm[:, vp * TT * HD + HD * kt:
                                    vp * TT * HD + HD * (kt + 1)],
                                start=(kt == 0), stop=(kt == qt))
                        return ins
                    return f
                sc.op("tensor", avm(), (f"ptb{tp}", f"vtm{vp}"), ("psav_F",),
                      tag=f"av{h}.{qt}")

            def s2c_v(h, qt):
                i4 = qt % 4
                sc.op("vector",
                      (lambda i4=i4: (lambda e: e.reciprocal(
                          linv[:, i4:i4 + 1], lsum[:, i4:i4 + 1])))(),
                      (f"lsum{i4}",), (f"linv{i4}",), tag=f"rcp{h}.{qt}")
                sc.op("vector",
                      (lambda i4=i4: (lambda e: e.tensor_scalar(
                          avtm[:, :], psav[:, 0:HD], linv[:, i4:i4 + 1],
                          None, op0=ALU.mult)))(),
                      ("psav_F", f"linv{i4}"), ("avtm",), tag=f"ave{h}.{qt}")

            def s2c_pe(h, qt):
                p = h % 2
                def atr():
                    def f(e):
                        ins = None
                        for hcc in range(HC):
                            ins = e.transpose(
                                psvt[:, 128 * hcc:128 * (hcc + 1)],
                                avtm[:, 128 * hcc:128 * (hcc + 1)],
                                id_sb[:, :])
                        return ins
                    return f
                sc.op("tensor", atr(), ("avtm", "id_sb"), ("psvt_F",),
                      tag=f"atr{h}.{qt}")
                def aev(qt=qt, p=p):
                    def f(e):
                        return e.tensor_copy(
                            aoh[:, p * HS:(p + 1) * HS]
                            .rearrange("q (c s) -> q c s", c=HC)
                            [:, :, 128 * qt:128 * (qt + 1)],
                            psvt[:, 0:128 * HC]
                            .rearrange("q (c f) -> q c f", c=HC))
                    return f
                sc.op("vector", aev(), ("psvt_F",), (f"aoh{p}",),
                      tag=f"aev{h}.{qt}")

            flat = [(h, qt) for h in range(H) for qt in range(TT)]
            LAG_A, LAG_B, LAG_C = 2, 3, 4
            head_loads(0)
            head_vtrans(0)
            head_loads(1)
            N = len(flat)
            for fi in range(N + LAG_C):
                if fi >= LAG_C:
                    s2c_v(*flat[fi - LAG_C])
                if fi < N:
                    h, qt = flat[fi]
                    s1(h, qt)
                    if qt == LAG_B and h >= 1 and h + 1 < H:
                        head_loads(h + 1)
                    if qt == LAG_B + 2 and h + 1 < H:
                        head_vtrans(h + 1)
                if fi >= LAG_A and fi - LAG_A < N:
                    s2a(*flat[fi - LAG_A])
                if fi >= LAG_C:
                    s2c_pe(*flat[fi - LAG_C])
                if fi >= LAG_B and fi - LAG_B < N:
                    s2b(*flat[fi - LAG_B])
                if fi < N and flat[fi][1] == LAG_C and flat[fi][0] >= 1:
                    head_spill(flat[fi][0] - 1)
            head_spill(H - 1)
            sc.end_block()

        OPW = min(512, D)
        NOP = D // OPW
        with contextlib.ExitStack() as G:
            b2b_sb = G.enter_context(nc.sbuf_tensor("b2b_sb", [128, D], F32))
            wonat = G.enter_context(nc.sbuf_tensor("wonat", [128, 2 * D], BF16))
            woTq = G.enter_context(nc.sbuf_tensor("woTq", [128, DC * OPW], BF16))
            v2q = G.enter_context(nc.sbuf_tensor("v2q", [128, RC * OPW], BF16))
            aoutg = G.enter_context(nc.sbuf_tensor("aoutg", [128, 2 * DC * 512],
                                                   BF16))
            xres = G.enter_context(nc.sbuf_tensor("xres", [128, 2 * OPW], F32))
            outsb = G.enter_context(nc.sbuf_tensor("outsb", [128, 2 * OPW], F32))
            psf = G.enter_context(nc.psum_tensor("psf_G", [128, 1024], F32))
            pswt = G.enter_context(nc.psum_tensor("pswt_G", [128, 2048], BF16))

            sc.dma("sync", b2b_sb[:, :], b2b_d[:, :], (), ("b2b_sb",), "ld_b2b")
            wo_d = w_attn["wo"]

            def ao_load(op_i, tg):
                gp = tg % 2
                sc.dma("sync",
                       aoutg[:, gp * DC * 512:(gp + 1) * DC * 512]
                       .rearrange("q (n f) -> q n f", n=DC),
                       aout_d[tg, :, :, :].rearrange("n q f -> q n f"),
                       (f"aoutg{gp}",), (f"aoutg{gp}",), f"aold{op_i}.{tg}")

            for op_i in range(NOP):
                for slab in range(OPW // 128):
                    oc = (OPW * op_i) // 128 + slab
                    p = slab % 2
                    wo_ap = wonat[:, p * D:(p + 1) * D]
                    sc.dma("gpsimd", wo_ap, wo_d[128 * oc:128 * (oc + 1), :],
                           (f"wonat{p}",), (f"wonat{p}",), f"wold{op_i}.{slab}")
                    for g in range((DC + 3) // 4):
                        ics = list(range(4 * g, min(4 * g + 4, DC)))
                        sl = g % 2
                        def em(wo_ap=wo_ap, ics=ics, sl=sl):
                            def f(e):
                                ins = None
                                for kk, ic in enumerate(ics):
                                    ins = e.transpose(
                                        pswt[:, 1024 * sl + 128 * kk:
                                             1024 * sl + 128 * (kk + 1)],
                                        wo_ap[:, 128 * ic:128 * (ic + 1)],
                                        id_sb[:, :])
                                return ins
                            return f
                        sc.op("tensor", em(), (f"wonat{p}", "id_sb"),
                              (f"pswt_G{sl}",), tag=f"wotr{op_i}.{slab}.{g}")
                        def ev(ics=ics, sl=sl, slab=slab):
                            def f(e):
                                return e.tensor_copy(
                                    woTq[:, :].rearrange(
                                        "q (c o) -> q c o", c=DC)
                                    [:, ics[0]:ics[0] + len(ics),
                                     128 * slab:128 * (slab + 1)],
                                    pswt[:, 1024 * sl:
                                         1024 * sl + 128 * len(ics)]
                                    .rearrange("q (c f) -> q c f", c=len(ics)))
                            return f
                        sc.op("vector", ev(), (f"pswt_G{sl}",),
                              ("woTq",), tag=f"woev{op_i}.{slab}.{g}")
                sc.dma("gpsimd",
                       v2q[:, :].rearrange("q (n o) -> q n o", n=RC),
                       v2_d[:, OPW * op_i:OPW * (op_i + 1)]
                       .rearrange("(n q) o -> q n o", q=128),
                       ("v2q",), ("v2q",), f"v2ld{op_i}")

                if op_i == 0:
                    ao_load(0, 0)
                for tt in range(TT):
                    tg = tt // 4
                    gp = tg % 2
                    if tt % 4 == 1 and tg + 1 < TC:
                        ao_load(op_i, tg + 1)
                    p = tt % 2
                    xr_ap = xres[:, p * OPW:(p + 1) * OPW]
                    ou_ap = outsb[:, p * OPW:(p + 1) * OPW]
                    sc.dma("sync", xr_ap,
                           x_d[128 * tt:128 * (tt + 1),
                               OPW * op_i:OPW * (op_i + 1)],
                           (f"xres{p}",), (f"xres{p}",), f"xrl{op_i}.{tt}")
                    sc.op("gpsimd",
                          (lambda xr_ap=xr_ap, op_i=op_i: (lambda e:
                              e.tensor_tensor(
                                  xr_ap, xr_ap,
                                  b2b_sb[:, OPW * op_i:OPW * (op_i + 1)],
                                  ALU.add)))(),
                          (f"xres{p}", "b2b_sb"), (f"xres{p}",),
                          tag=f"xb2.{op_i}.{tt}")
                    def fmm(tt=tt, p=p, gp=gp):
                        def f(e):
                            ins = None
                            for i in range(DC):
                                ins = e.matmul(
                                    psf[:, 512 * p:512 * (p + 1)],
                                    aoutg[:, gp * DC * 512 + 512 * i
                                          + 128 * (tt % 4):
                                          gp * DC * 512 + 512 * i
                                          + 128 * (tt % 4 + 1)],
                                    woTq[:, OPW * i:OPW * i + 512],
                                    start=(i == 0), stop=False)
                            for r in range(RC):
                                ins = e.matmul(
                                    psf[:, 512 * p:512 * (p + 1)],
                                    zbuf[:, r * S + 128 * tt:
                                         r * S + 128 * (tt + 1)],
                                    v2q[:, OPW * r:OPW * r + 512],
                                    start=False, stop=(r == RC - 1))
                            return ins
                        return f
                    sc.op("tensor", fmm(), (f"aoutg{gp}", "woTq", "zbuf",
                                            "v2q"),
                          (f"psf_G{p}",), tag=f"fmm{op_i}.{tt}")
                    sc.op("vector",
                          (lambda xr_ap=xr_ap, ou_ap=ou_ap, p=p: (lambda e:
                              e.scalar_tensor_tensor(
                                  ou_ap, xr_ap, 1.0,
                                  psf[:, 512 * p:512 * (p + 1)],
                                  op0=ALU.mult, op1=ALU.add)))(),
                          (f"xres{p}", f"psf_G{p}"), (f"outsb{p}",),
                          tag=f"oev{op_i}.{tt}")
                    sc.dma("sync",
                           out_d[128 * tt:128 * (tt + 1),
                                 OPW * op_i:OPW * (op_i + 1)], ou_ap,
                           (f"outsb{p}",), ("OUT",), f"osp{op_i}.{tt}")
                if op_i + 1 < NOP:
                    ao_load(op_i + 1, 0)
            sc.end_block()
        st_z.close()

    return nc



def make_inmaps(cfg, roles, x_full):
    consts = make_consts(cfg)
    b1 = np.asarray(roles["b1"], dtype=np.float32)
    b2 = np.asarray(roles["b2"], dtype=np.float32)
    base = {k: np.ascontiguousarray(np.asarray(roles[k], dtype=np.float32))
            for k in ("wq", "wk", "wv", "wo", "u1", "v1", "u2", "v2")}
    base["b1pt"] = np.ascontiguousarray(b1.reshape(2 * cfg.IC, 128).T)
    base["b2b"] = np.ascontiguousarray(
        np.broadcast_to(b2, (128, cfg.D)).astype(np.float32))
    base.update({k: np.ascontiguousarray(v) for k, v in consts.items()})
    base["epsv"] = np.full((128, 1), cfg.eps, dtype=np.float32)
    return [dict(base, x=np.ascontiguousarray(x_full[b], dtype=np.float32))
            for b in range(x_full.shape[0])]


def kernel(L0_attn_hidden_states, L0_attn_wq, L0_attn_wk, L0_attn_wv,
           L0_attn_wo, L0_ffn_U1, L0_ffn_V1, L0_ffn_U2, L0_ffn_V2,
           L0_ffn_b1, L0_ffn_b2):
    cfg = CFG_FULL
    x = np.asarray(L0_attn_hidden_states, dtype=np.float32)
    B = x.shape[0]
    roles = {"wq": L0_attn_wq, "wk": L0_attn_wk, "wv": L0_attn_wv,
             "wo": L0_attn_wo, "u1": L0_ffn_U1, "v1": L0_ffn_V1,
             "u2": L0_ffn_U2, "v2": L0_ffn_V2, "b1": L0_ffn_b1,
             "b2": L0_ffn_b2}
    nc = build_nc(cfg)
    in_maps = make_inmaps(cfg, roles, x)
    res = run_bass_kernel_spmd(nc, in_maps, core_ids=list(range(B)))
    return np.stack([res.results[b]["out"] for b in range(B)], axis=0)


# version 15 (final)
# speedup vs baseline: 1.1099x; 1.0088x over previous
import math
import contextlib

import numpy as np
import ml_dtypes

import concourse.bass as bass
import concourse.mybir as mybir
from concourse.alu_op_type import AluOpType as ALU
from concourse.bass_utils import run_bass_kernel_spmd

F32 = mybir.dt.float32
BF16 = mybir.dt.bfloat16
AF = mybir.ActivationFunctionType
AX = mybir.AxisListType
bfdt = ml_dtypes.bfloat16

NEG = -1.0e9


class Cfg:
    def __init__(self, S=2048, D=4096, H=16, HD=256, ROT=64, RANK=1024,
                 INTER=11008, eps=1e-5):
        self.S, self.D, self.H, self.HD, self.ROT = S, D, H, HD, ROT
        self.RANK, self.INTER, self.eps = RANK, INTER, eps
        self.DC = D // 128
        self.TT = S // 128
        self.TC = S // 512
        self.RC = RANK // 128
        self.IC = INTER // 128
        self.HC = HD // 128
        self.scale = 1.0 / math.sqrt(HD)


CFG_FULL = Cfg()



def make_consts(cfg):
    S, ROT = cfg.S, cfg.ROT
    ident = np.eye(128, dtype=bfdt)
    rt = np.zeros((128, 128), dtype=np.float32)
    for i in range(ROT // 2):
        rt[2 * i + 1, 2 * i] = -1.0
        rt[2 * i, 2 * i + 1] = 1.0
    rt = rt.astype(bfdt)
    inv_freq = 1.0 / (10000.0 ** (np.arange(0, ROT, 2, dtype=np.float32) / ROT))
    ang = inv_freq[:, None] * np.arange(S, dtype=np.float32)[None, :]
    cosx = np.ones((128, S), dtype=np.float32)
    sinx = np.zeros((128, S), dtype=np.float32)
    for d in range(ROT):
        cosx[d, :] = np.cos(ang[d // 2, :])
        sinx[d, :] = np.sin(ang[d // 2, :])
    masks = np.full((128, 5 * 512), NEG, dtype=np.float32)
    for v in range(4):
        for p in range(128):
            masks[p, v * 512: v * 512 + min(512, 128 * v + p + 1)] = 0.0
    masks[:, 4 * 512:] = 0.0
    return {"ident": ident, "rt": rt, "cosx": cosx, "sinx": sinx,
            "masks": masks}



ENGS = ("sync", "gpsimd", "tensor", "vector", "scalar")
DMA_QUEUES = ("sync", "gpsimd")


class Rec:
    __slots__ = ("eng", "emit", "reads", "writes", "dma", "idx", "deps",
                 "need_inc", "inc_val", "tag", "drain_before", "waits",
                 "semkey")

    def __init__(self, eng, emit, reads, writes, dma, idx, tag):
        self.eng, self.emit = eng, emit
        self.reads, self.writes, self.dma = tuple(reads), tuple(writes), dma
        self.idx, self.tag = idx, tag
        self.deps = set()
        self.need_inc = dma
        self.inc_val = None
        self.semkey = None
        self.drain_before = False
        self.waits = []


NLANES = 8


class Sched:
    def __init__(self, nc):
        self.nc = nc
        self.recs = []
        self.writers = {}
        self.readers = {}
        self.sems = {}
        self.counters = {e + "_cmp": 0 for e in ENGS}
        for q in DMA_QUEUES:
            for l in range(NLANES):
                self.counters[f"{q}_dma{l}"] = 0
        self.dma_n = {q: 0 for q in DMA_QUEUES}
        self.nblocks = 0

    def op(self, eng, emit, reads=(), writes=(), dma=False, tag=""):
        r = Rec(eng, emit, reads, writes, dma, len(self.recs), tag)
        if dma:
            lane = self.dma_n[eng] % NLANES
            self.dma_n[eng] += 1
            r.semkey = f"{eng}_dma{lane}"
            self.counters[r.semkey] += 16
            r.inc_val = self.counters[r.semkey]
        else:
            r.semkey = eng + "_cmp"
        for b in r.reads:
            for w in self.writers.get(b, ()):
                r.deps.add(w)
        for b in r.writes:
            for w in self.writers.get(b, ()):
                r.deps.add(w)
            for rd in self.readers.get(b, ()):
                r.deps.add(rd)
        r.deps.discard(r)
        self.recs.append(r)
        for b in r.reads:
            self.readers.setdefault(b, []).append(r)
        for b in r.writes:
            self.writers.setdefault(b, []).append(r)
        return r

    def dma(self, eng, out_ap, in_ap, reads=(), writes=(), tag=""):
        self.op(eng, (lambda o=out_ap, i=in_ap: (lambda e: e.dma_start(o, i)))(),
                reads=reads, writes=writes, dma=True, tag=tag)

    def end_block(self):
        nc = self.nc
        recs = self.recs
        pos = {}
        stream_len = {e: 0 for e in ENGS}
        for r in recs:
            pos[r] = stream_len[r.eng]
            stream_len[r.eng] += 1
        drain_cover = {e: -1 for e in ENGS}
        for r in recs:
            for d in sorted(r.deps, key=lambda x: x.idx):
                if d.dma:
                    d.need_inc = True
                    r.waits.append(d)
                elif d.eng == r.eng:
                    if r.eng == "tensor":
                        continue
                    if pos[d] > drain_cover[r.eng]:
                        r.drain_before = True
                        drain_cover[r.eng] = pos[r] - 1
                else:
                    d.need_inc = True
                    r.waits.append(d)
        for r in recs:
            if r.need_inc and not r.dma:
                self.counters[r.semkey] += 1
                r.inc_val = self.counters[r.semkey]

        sems = self.sems
        counters = dict(self.counters)

        with nc.Block(f"blk{self.nblocks}") as blk:
            for eng_name in ENGS:
                myrecs = [r for r in recs if r.eng == eng_name]
                flushes = []
                if eng_name in DMA_QUEUES:
                    for l in range(NLANES):
                        k = f"{eng_name}_dma{l}"
                        if counters[k] > 0:
                            flushes.append((sems[k], counters[k]))
                if not myrecs and not flushes:
                    continue

                def make_section(myrecs=myrecs, flushes=flushes):
                    def section(e):
                        watermark = {}
                        for r in myrecs:
                            if r.drain_before:
                                e.drain()
                            best = {}
                            for d in r.waits:
                                if best.get(d.semkey, -1) < d.inc_val:
                                    best[d.semkey] = d.inc_val
                            if r.dma and r.inc_val > 16:
                                if best.get(r.semkey, -1) < r.inc_val - 16:
                                    best[r.semkey] = r.inc_val - 16
                            for k in sorted(best):
                                v = best[k]
                                if watermark.get(k, -1) >= v:
                                    continue
                                e.wait_ge(sems[k], v)
                                watermark[k] = v
                            instr = r.emit(e)
                            if r.need_inc:
                                if instr is None:
                                    raise RuntimeError("no instr: " + r.tag)
                                instr.then_inc(sems[r.semkey],
                                               16 if r.dma else 1)
                        for (s, v) in flushes:
                            e.wait_ge(s, v)
                    return section

                getattr(blk, eng_name)(make_section())

        self.recs = []
        self.writers = {}
        self.readers = {}
        self.nblocks += 1



def build_nc(cfg, debug=False):
    nc = bass.Bass("TRN2", target_bir_lowering=False)
    S, D, H, HD, ROT, RANK, INTER = (cfg.S, cfg.D, cfg.H, cfg.HD, cfg.ROT,
                                     cfg.RANK, cfg.INTER)
    DC, TT, TC, RC, IC, HC = cfg.DC, cfg.TT, cfg.TC, cfg.RC, cfg.IC, cfg.HC

    x_d = nc.dram_tensor("x", [S, D], F32, kind="ExternalInput")
    w_attn = {n: nc.dram_tensor(n, [D, D], F32, kind="ExternalInput")
              for n in ("wq", "wk", "wv", "wo")}
    u1_d = nc.dram_tensor("u1", [D, RANK], F32, kind="ExternalInput")
    v1_d = nc.dram_tensor("v1", [RANK, 2 * INTER], F32, kind="ExternalInput")
    u2_d = nc.dram_tensor("u2", [INTER, RANK], F32, kind="ExternalInput")
    v2_d = nc.dram_tensor("v2", [RANK, D], F32, kind="ExternalInput")
    b1pt_d = nc.dram_tensor("b1pt", [128, 2 * IC], F32, kind="ExternalInput")
    b2b_d = nc.dram_tensor("b2b", [128, D], F32, kind="ExternalInput")
    ident_d = nc.dram_tensor("ident", [128, 128], BF16, kind="ExternalInput")
    rt_d = nc.dram_tensor("rt", [128, 128], BF16, kind="ExternalInput")
    cosx_d = nc.dram_tensor("cosx", [128, S], F32, kind="ExternalInput")
    sinx_d = nc.dram_tensor("sinx", [128, S], F32, kind="ExternalInput")
    masks_d = nc.dram_tensor("masks", [128, 5 * 512], F32, kind="ExternalInput")
    epsv_d = nc.dram_tensor("epsv", [128, 1], F32, kind="ExternalInput")

    out_d = nc.dram_tensor("out", [S, D], F32, kind="ExternalOutput")
    dk = dict(kind="ExternalOutput") if debug else {}
    qf_d = nc.dram_tensor("qf", [D, S], BF16, **dk)
    kf_d = nc.dram_tensor("kf", [D, S], BF16, **dk)
    vf_d = nc.dram_tensor("vf", [D, S], BF16, **dk)
    act_d = nc.dram_tensor("actsp", [IC * 128, S], BF16, **dk)
    u2bf_d = nc.dram_tensor("u2bf", [INTER, RANK], BF16)
    aout_d = nc.dram_tensor("aoutsp", [TC, DC, 128, 512], BF16, **dk)

    sc = Sched(nc)

    with contextlib.ExitStack() as ctx:
        semkeys = [e + "_cmp" for e in ENGS]
        for q in DMA_QUEUES:
            semkeys += [f"{q}_dma{l}" for l in range(NLANES)]
        for key in semkeys:
            sc.sems[key] = ctx.enter_context(nc.semaphore(key))

        id_sb = ctx.enter_context(nc.sbuf_tensor("id_sb", [128, 128], BF16))
        eps_sb = ctx.enter_context(nc.sbuf_tensor("eps_sb", [128, 1], F32))
        rt_sb = ctx.enter_context(nc.sbuf_tensor("rt_sb", [128, 128], BF16))
        b1_sb = ctx.enter_context(nc.sbuf_tensor("b1_sb", [128, 2 * IC], F32))

        st_xk = contextlib.ExitStack()
        st_cs = contextlib.ExitStack()
        st_psl = contextlib.ExitStack()
        st_z = contextlib.ExitStack()

        xk = st_xk.enter_context(nc.sbuf_tensor("xk", [128, DC * S], BF16))
        cos_sb = st_cs.enter_context(nc.sbuf_tensor("cos_sb", [128, S], F32))
        sin_sb = st_cs.enter_context(nc.sbuf_tensor("sin_sb", [128, S], F32))

        with contextlib.ExitStack() as A:
            x_in = A.enter_context(nc.sbuf_tensor("x_in", [128, 2 * D], F32))
            xn_tm = A.enter_context(nc.sbuf_tensor("xn_tm", [128, 2 * D], BF16))
            nstat = D // 512
            stats = A.enter_context(nc.sbuf_tensor("stats", [128, nstat * 6], F32))
            mv = A.enter_context(nc.sbuf_tensor("mv", [128, 2], F32))
            rstd = A.enter_context(nc.sbuf_tensor("rstd", [128, 1], F32))
            nm = A.enter_context(nc.sbuf_tensor("nm", [128, 1], F32))
            psT = A.enter_context(nc.psum_tensor("psT_A", [128, 2048], BF16))

            sc.dma("sync", id_sb[:, :], ident_d[:, :], (), ("id_sb",), "ld_id")
            sc.dma("sync", eps_sb[:, :], epsv_d[:, :], (), ("eps_sb",), "ld_eps")
            sc.dma("sync", rt_sb[:, :], rt_d[:, :], (), ("rt_sb",), "ld_rt")
            sc.dma("sync", cos_sb[:, :], cosx_d[:, :], (), ("cos_sb",), "ld_cos")
            sc.dma("sync", sin_sb[:, :], sinx_d[:, :], (), ("sin_sb",), "ld_sin")
            sc.dma("sync", b1_sb[:, :], b1pt_d[:, :], (), ("b1_sb",), "ld_b1")

            for tt in range(TT):
                p = tt % 2
                xi = x_in[:, p * D:(p + 1) * D]
                xn = xn_tm[:, p * D:(p + 1) * D]
                sc.dma("sync", xi, x_d[128 * tt:128 * (tt + 1), :],
                       (f"x_in{p}",), (f"x_in{p}",), f"ldx{tt}")
                for j in range(nstat):
                    sc.op("vector",
                          (lambda xi=xi, j=j: (lambda e: e.bn_stats(
                              stats[:, 6 * j:6 * (j + 1)],
                              xi[:, 512 * j:512 * (j + 1)])))(),
                          (f"x_in{p}",), ("stats",), tag=f"bnst{tt}.{j}")
                sc.op("vector",
                      (lambda: (lambda e: e.bn_aggr(mv[:, :], stats[:, :])))(),
                      ("stats",), ("mv",), tag=f"bnag{tt}")
                sc.op("scalar",
                      (lambda: (lambda e: e.activation(
                          rstd[:, :], mv[:, 1:2], AF.Ln,
                          bias=eps_sb[:, :])))(),
                      ("mv", "eps_sb"), ("rstd",), tag=f"lnv{tt}")
                sc.op("scalar",
                      (lambda: (lambda e: e.activation(
                          rstd[:, :], rstd[:, :], AF.Exp, scale=-0.5)))(),
                      ("rstd",), ("rstd",), tag=f"expv{tt}")
                sc.op("vector",
                      (lambda: (lambda e: e.tensor_tensor(
                          nm[:, :], mv[:, 0:1], rstd[:, :], ALU.mult)))(),
                      ("mv", "rstd"), ("nm",), tag=f"nm{tt}")
                sc.op("vector",
                      (lambda xi=xi, xn=xn: (lambda e: e.tensor_scalar(
                          xn, xi, rstd[:, :], nm[:, :],
                          op0=ALU.mult, op1=ALU.subtract)))(),
                      (f"x_in{p}", "rstd", "nm"), (f"xn_tm{p}",), tag=f"nrm{tt}")
                for g in range((DC + 3) // 4):
                    ics = list(range(4 * g, min(4 * g + 4, DC)))
                    sl = g % 2
                    def em(xn=xn, ics=ics, sl=sl):
                        def f(e):
                            ins = None
                            for k, ic in enumerate(ics):
                                ins = e.transpose(
                                    psT[:, 1024 * sl + 128 * k:
                                        1024 * sl + 128 * (k + 1)],
                                    xn[:, 128 * ic:128 * (ic + 1)], id_sb[:, :])
                            return ins
                        return f
                    sc.op("tensor", em(), (f"xn_tm{p}", "id_sb"), (f"psT_A{sl}",),
                          tag=f"xtr{tt}.{g}")
                    def ev(tt=tt, ics=ics, sl=sl):
                        def f(e):
                            return e.tensor_copy(
                                xk[:, :].rearrange("q (c s) -> q c s", c=DC)
                                [:, ics[0]:ics[0] + len(ics),
                                 128 * tt:128 * (tt + 1)],
                                psT[:, 1024 * sl:1024 * sl + 128 * len(ics)]
                                .rearrange("q (c s) -> q c s", c=len(ics)))
                        return f
                    sc.op("vector", ev(), (f"psT_A{sl}",), ("xk",),
                          tag=f"xev{tt}.{g}")
            sc.end_block()

        with contextlib.ExitStack() as B:
            wnat = B.enter_context(nc.sbuf_tensor("wnat", [128, 2 * D], BF16))
            wT = B.enter_context(nc.sbuf_tensor("wT", [128, 2 * D], BF16))
            qsb = B.enter_context(nc.sbuf_tensor("qsb", [128, 2 * S], BF16))
            qout = B.enter_context(nc.sbuf_tensor("qout", [128, 2 * S], BF16))
            trope = B.enter_context(nc.sbuf_tensor("trope", [128, 1024], F32))
            psa = B.enter_context(nc.psum_tensor("psa_B", [128, S], F32))
            psb = B.enter_context(nc.psum_tensor("psb_B", [128, 1024], F32))
            psTB = B.enter_context(nc.psum_tensor("psT_B", [128, 2048], BF16))

            half = IC * 128 // 2
            sc.dma("gpsimd", u2bf_d[0:half, :], u2_d[0:half, :],
                   (), ("U2BF",), "u2cast0")
            sc.dma("gpsimd", u2bf_d[half:, :], u2_d[half:, :],
                   (), ("U2BF",), "u2cast1")

            WLIST = [("wq", qf_d, 0), ("wk", kf_d, 1), ("wv", vf_d, 2)]
            jobs = [(wi, oc) for wi in range(3) for oc in range(DC)]

            def wload(j):
                wi, oc = jobs[j]
                p = j % 2
                sc.dma("gpsimd", wnat[:, p * D:(p + 1) * D],
                       w_attn[WLIST[wi][0]][128 * oc:128 * (oc + 1), :],
                       (f"wnat{p}",), (f"wnat{p}",), f"wld{wi}.{oc}")

            def wtrans(j):
                wi, oc = jobs[j]
                p = j % 2
                wn_ap = wnat[:, p * D:(p + 1) * D]
                wT_ap = wT[:, p * D:(p + 1) * D]
                for g in range((DC + 3) // 4):
                    ics = list(range(4 * g, min(4 * g + 4, DC)))
                    sl = g % 2
                    def em(wn_ap=wn_ap, ics=ics, sl=sl):
                        def f(e):
                            ins = None
                            for kk, ic in enumerate(ics):
                                ins = e.transpose(
                                    psTB[:, 1024 * sl + 128 * kk:
                                         1024 * sl + 128 * (kk + 1)],
                                    wn_ap[:, 128 * ic:128 * (ic + 1)],
                                    id_sb[:, :])
                            return ins
                        return f
                    sc.op("tensor", em(), (f"wnat{p}", "id_sb"),
                          (f"psT_B{sl}",), tag=f"wtr{wi}.{oc}.{g}")
                    def ev(wT_ap=wT_ap, ics=ics, sl=sl):
                        def f(e):
                            return e.tensor_copy(
                                wT_ap[:, 128 * ics[0]:
                                      128 * (ics[0] + len(ics))],
                                psTB[:, 1024 * sl:1024 * sl + 128 * len(ics)])
                        return f
                    sc.op("vector", ev(), (f"psT_B{sl}",), (f"wT{p}",),
                          tag=f"wev{wi}.{oc}.{g}")

            rope_pend = []
            rp_ctr = [0]

            def emit_rope(wi, oc, rp, p):
                qo_ap = qout[:, p * S:(p + 1) * S]
                qs_ap = qsb[:, rp * S:rp * S + S]
                for tc in range(TC):
                    bk = tc % 2
                    sc.op("tensor",
                          (lambda tc=tc, bk=bk, qs_ap=qs_ap: (lambda e:
                              e.matmul(
                                  psb[:, 512 * bk:512 * (bk + 1)],
                                  rt_sb[:, :],
                                  qs_ap[:, 512 * tc:512 * (tc + 1)],
                                  start=True, stop=True)))(),
                          ("rt_sb", f"qsb{rp}"), (f"psb_B{bk}",),
                          tag=f"rq{wi}.{oc}.{tc}")
                    sc.op("vector",
                          (lambda tc=tc, bk=bk, qs_ap=qs_ap: (lambda e:
                              e.tensor_tensor(
                                  trope[:, 512 * bk:512 * (bk + 1)],
                                  qs_ap[:, 512 * tc:512 * (tc + 1)],
                                  cos_sb[:, 512 * tc:512 * (tc + 1)],
                                  ALU.mult)))(),
                          (f"qsb{rp}", "cos_sb"), (f"trope{bk}",),
                          tag=f"rc1.{wi}.{oc}.{tc}")
                    sc.op("vector",
                          (lambda tc=tc, bk=bk, qo_ap=qo_ap: (lambda e:
                              e.scalar_tensor_tensor(
                                  qo_ap[:, 512 * tc:512 * (tc + 1)],
                                  psb[:, 512 * bk:512 * (bk + 1)], 1.0,
                                  sin_sb[:, 512 * tc:512 * (tc + 1)],
                                  op0=ALU.mult, op1=ALU.mult)))(),
                          (f"psb_B{bk}", "sin_sb"), (f"qout{p}",),
                          tag=f"rc2.{wi}.{oc}.{tc}")
                    sc.op("vector",
                          (lambda tc=tc, bk=bk, qo_ap=qo_ap: (lambda e:
                              e.tensor_tensor(
                                  qo_ap[:, 512 * tc:512 * (tc + 1)],
                                  qo_ap[:, 512 * tc:512 * (tc + 1)],
                                  trope[:, 512 * bk:512 * (bk + 1)],
                                  ALU.add)))(),
                          (f"trope{bk}", f"qout{p}"), (f"qout{p}",),
                          tag=f"rc3.{wi}.{oc}.{tc}")
                sc.dma("sync",
                       WLIST[wi][1][128 * oc:128 * (oc + 1), :], qo_ap,
                       (f"qout{p}",), (f"OF{wi}",), f"qsp{wi}.{oc}")

            wload(0)
            wload(1)
            wtrans(0)
            for j, (wi, oc) in enumerate(jobs):
                p = j % 2
                wT_ap = wT[:, p * D:(p + 1) * D]
                qo_ap = qout[:, p * S:(p + 1) * S]
                rope = (wi < 2) and (oc % HC == 0) and ROT > 0
                if j + 2 < len(jobs):
                    wload(j + 2)
                if j + 1 < len(jobs):
                    wtrans(j + 1)
                while rope_pend:
                    emit_rope(*rope_pend.pop(0))
                for tc in range(TC):
                    def mgq(wT_ap=wT_ap, tc=tc):
                        def f(e):
                            ins = None
                            for i in range(DC):
                                ins = e.matmul(
                                    psa[:, 512 * tc:512 * (tc + 1)],
                                    wT_ap[:, 128 * i:128 * (i + 1)],
                                    xk[:, i * S + 512 * tc:
                                       i * S + 512 * (tc + 1)],
                                    start=(i == 0), stop=(i == DC - 1))
                            return ins
                        return f
                    sc.op("tensor", mgq(), (f"wT{p}", "xk"),
                          (f"psa_B{tc}",), tag=f"mm{wi}.{oc}.{tc}")
                if not rope:
                    for tc in range(TC):
                        sc.op("vector",
                              (lambda qo_ap=qo_ap, tc=tc: (lambda e:
                                  e.tensor_copy(
                                      qo_ap[:, 512 * tc:512 * (tc + 1)],
                                      psa[:, 512 * tc:512 * (tc + 1)])))(),
                              (f"psa_B{tc}",), (f"qout{p}",),
                              tag=f"qev{wi}.{oc}.{tc}")
                    sc.dma("sync",
                           WLIST[wi][1][128 * oc:128 * (oc + 1), :], qo_ap,
                           (f"qout{p}",), (f"OF{wi}",), f"qsp{wi}.{oc}")
                else:
                    rp = rp_ctr[0] % 2
                    rp_ctr[0] += 1
                    qs_ap = qsb[:, rp * S:rp * S + S]
                    for tc in range(TC):
                        sc.op("vector",
                              (lambda qs_ap=qs_ap, tc=tc: (lambda e:
                                  e.tensor_copy(
                                      qs_ap[:, 512 * tc:512 * (tc + 1)],
                                      psa[:, 512 * tc:512 * (tc + 1)])))(),
                              (f"psa_B{tc}",), (f"qsb{rp}",),
                              tag=f"qevr{wi}.{oc}.{tc}")
                    rope_pend.append((wi, oc, rp, p))
            while rope_pend:
                emit_rope(*rope_pend.pop(0))
            sc.end_block()
        st_cs.close()

        psl = st_psl.enter_context(nc.sbuf_tensor("psl", [128, RC * S], BF16))
        with contextlib.ExitStack() as C:
            u1sl = C.enter_context(nc.sbuf_tensor("u1sl", [128, 2 * DC * 128],
                                                  BF16))
            psaC = C.enter_context(nc.psum_tensor("psa_C", [128, S], F32))
            for rc in range(RC):
                p = rc % 2
                u1_ap = u1sl[:, p * DC * 128:(p + 1) * DC * 128]
                sc.dma("gpsimd",
                       u1_ap.rearrange("q (n r) -> q n r", n=DC),
                       u1_d[:, 128 * rc:128 * (rc + 1)]
                       .rearrange("(n q) r -> q n r", q=128),
                       (f"u1sl{p}",), (f"u1sl{p}",), f"u1ld{rc}")
                for tc in range(TC):
                    def mgq(u1_ap=u1_ap, tc=tc):
                        def f(e):
                            ins = None
                            for i in range(DC):
                                ins = e.matmul(
                                    psaC[:, 512 * tc:512 * (tc + 1)],
                                    u1_ap[:, 128 * i:128 * (i + 1)],
                                    xk[:, i * S + 512 * tc:
                                       i * S + 512 * (tc + 1)],
                                    start=(i == 0), stop=(i == DC - 1))
                            return ins
                        return f
                    sc.op("tensor", mgq(), (f"u1sl{p}", "xk"),
                          (f"psa_C{tc}",), tag=f"pmm{rc}.{tc}")
                    sc.op("vector",
                          (lambda rc=rc, tc=tc: (lambda e: e.tensor_copy(
                              psl[:, rc * S + 512 * tc:
                                  rc * S + 512 * (tc + 1)],
                              psaC[:, 512 * tc:512 * (tc + 1)])))(),
                          (f"psa_C{tc}",), ("psl",), tag=f"pev{rc}.{tc}")
            sc.end_block()

        with contextlib.ExitStack() as Dk:
            v1g = Dk.enter_context(nc.sbuf_tensor("v1g", [128, 2 * RC * 128], BF16))
            v1u = Dk.enter_context(nc.sbuf_tensor("v1u", [128, 2 * RC * 128], BF16))
            silu_f = Dk.enter_context(nc.sbuf_tensor("silu_f", [128, S], F32))
            up_f = Dk.enter_context(nc.sbuf_tensor("up_f", [128, S], F32))
            actst = Dk.enter_context(nc.sbuf_tensor("actst", [128, 2 * S], BF16))
            psg = Dk.enter_context(nc.psum_tensor("psg_D", [128, S], F32))
            psu = Dk.enter_context(nc.psum_tensor("psu_D", [128, S], F32))
            W = RC * 128
            for c in range(IC):
                p = c % 2
                g_ap = v1g[:, p * W:(p + 1) * W]
                u_ap = v1u[:, p * W:(p + 1) * W]
                a_ap = actst[:, p * S:(p + 1) * S]
                sc.dma("gpsimd", g_ap.rearrange("q (n f) -> q n f", n=RC),
                       v1_d[:, 128 * c:128 * (c + 1)]
                       .rearrange("(n q) f -> q n f", q=128),
                       (f"v1g{p}",), (f"v1g{p}",), f"v1gld{c}")
                sc.dma("gpsimd", u_ap.rearrange("q (n f) -> q n f", n=RC),
                       v1_d[:, 128 * (IC + c):128 * (IC + c + 1)]
                       .rearrange("(n q) f -> q n f", q=128),
                       (f"v1u{p}",), (f"v1u{p}",), f"v1uld{c}")
                for which, w_ap, ps, nm_ in (("g", g_ap, psg, "psg_D"),
                                             ("u", u_ap, psu, "psu_D")):
                    def mg(w_ap=w_ap, ps=ps):
                        def f(e):
                            ins = None
                            for r in range(RC):
                                for tc in range(TC):
                                    ins = e.matmul(
                                        ps[:, 512 * tc:512 * (tc + 1)],
                                        w_ap[:, 128 * r:128 * (r + 1)],
                                        psl[:, r * S + 512 * tc:
                                            r * S + 512 * (tc + 1)],
                                        start=(r == 0), stop=(r == RC - 1))
                            return ins
                        return f
                    sc.op("tensor", mg(),
                          (f"v1{which}{p}", "psl"), (nm_,), tag=f"h{which}{c}")
                sc.op("scalar",
                      (lambda c=c: (lambda e: e.activation(
                          silu_f[:, :], psg[:, :], AF.Silu,
                          bias=b1_sb[:, c:c + 1])))(),
                      ("psg_D", "b1_sb"), ("silu_f",), tag=f"sil{c}")
                sc.op("scalar",
                      (lambda c=c: (lambda e: e.activation(
                          up_f[:, :], psu[:, :], AF.Identity,
                          bias=b1_sb[:, IC + c:IC + c + 1])))(),
                      ("psu_D", "b1_sb"), ("up_f",), tag=f"up{c}")
                sc.op("vector",
                      (lambda a_ap=a_ap: (lambda e: e.tensor_tensor(
                          a_ap, silu_f[:, :], up_f[:, :], ALU.mult)))(),
                      ("silu_f", "up_f"), (f"actst{p}",), tag=f"am{c}")
                sc.dma("sync", act_d[128 * c:128 * (c + 1), :], a_ap,
                       (f"actst{p}",), ("ACT",), f"asp{c}")
            sc.end_block()
        st_psl.close()
        st_xk.close()

        zbuf = st_z.enter_context(nc.sbuf_tensor("zbuf", [128, RC * S], BF16))
        with contextlib.ExitStack() as E:
            u2row = E.enter_context(nc.sbuf_tensor("u2row", [128, 3 * RANK], BF16))
            acti = E.enter_context(nc.sbuf_tensor("acti", [128, 3 * 512], BF16))
            psz = E.enter_context(nc.psum_tensor("psz_E", [128, RC * 512], F32))
            for tc in range(TC):
                for ic in range(IC):
                    p = ic % 3
                    u2_ap = u2row[:, p * RANK:(p + 1) * RANK]
                    ac_ap = acti[:, p * 512:(p + 1) * 512]
                    sc.dma("sync", u2_ap, u2bf_d[128 * ic:128 * (ic + 1), :],
                           (f"u2row{p}",), (f"u2row{p}",), f"u2ld{tc}.{ic}")
                    sc.dma("gpsimd", ac_ap,
                           act_d[128 * ic:128 * (ic + 1),
                                 512 * tc:512 * (tc + 1)],
                           (f"acti{p}",), (f"acti{p}",), f"ald{tc}.{ic}")
                    def mg(u2_ap=u2_ap, ac_ap=ac_ap, ic=ic):
                        def f(e):
                            ins = None
                            for r in range(RC):
                                ins = e.matmul(
                                    psz[:, 512 * r:512 * (r + 1)],
                                    u2_ap[:, 128 * r:128 * (r + 1)], ac_ap,
                                    start=(ic == 0), stop=(ic == IC - 1))
                            return ins
                        return f
                    sc.op("tensor", mg(), (f"u2row{p}", f"acti{p}"), ("psz_E",),
                          tag=f"zmm{tc}.{ic}")
                sc.op("vector",
                      (lambda tc=tc: (lambda e: e.tensor_copy(
                          zbuf[:, :].rearrange("q (r s) -> q r s", r=RC)
                          [:, :, 512 * tc:512 * (tc + 1)],
                          psz[:, :].rearrange("q (r f) -> q r f", r=RC))))(),
                      ("psz_E",), ("zbuf",), tag=f"zev{tc}")
            sc.end_block()

        with contextlib.ExitStack() as Fk:
            mask_sb = Fk.enter_context(nc.sbuf_tensor("mask_sb", [128, 5 * 512], F32))
            qh = Fk.enter_context(nc.sbuf_tensor("qh", [128, 2 * HC * S], BF16))
            kh = Fk.enter_context(nc.sbuf_tensor("kh", [128, 2 * HC * S], BF16))
            vhf = Fk.enter_context(nc.sbuf_tensor("vhf", [128, 2 * HC * S], BF16))
            vtm = Fk.enter_context(nc.sbuf_tensor("vtm", [128, 2 * TT * HD], BF16))
            aoh = Fk.enter_context(nc.sbuf_tensor("aoh", [128, 2 * HC * S], BF16))
            pbuf = Fk.enter_context(nc.sbuf_tensor("pbuf", [128, 4 * S], BF16))
            SW = 512 * ((TT - 1) // 4 + 1)
            sfb_t = Fk.enter_context(nc.sbuf_tensor("sfb", [128, 2 * SW], F32))
            ptb = Fk.enter_context(nc.sbuf_tensor("ptb", [128, 2 * S], BF16))
            avtm = Fk.enter_context(nc.sbuf_tensor("avtm", [128, HD], BF16))
            mrun = Fk.enter_context(nc.sbuf_tensor("mrun", [128, 1], F32))
            bias16 = Fk.enter_context(nc.sbuf_tensor("bias16", [128, 4], F32))
            lsum = Fk.enter_context(nc.sbuf_tensor("lsum", [128, 4], F32))
            linv = Fk.enter_context(nc.sbuf_tensor("linv", [128, 4], F32))
            pss = Fk.enter_context(nc.psum_tensor("pss_F", [128, 512 * ((TT - 1) // 4 + 1)], F32))
            pspt = Fk.enter_context(nc.psum_tensor("pst_F", [128, 2048], BF16))
            psav = Fk.enter_context(nc.psum_tensor("psav_F", [128, 512], F32))
            psvt = Fk.enter_context(nc.psum_tensor("psvt_F", [128, 512], BF16))

            sc.dma("sync", mask_sb[:, :], masks_d[:, :], (), ("mask_sb",),
                   "ld_mask")
            HS = HC * S

            def head_loads(h):
                p = h % 2
                for nmq, off, src_d, bufn in (
                        (f"qh{p}", 0, qf_d, "OF0"),
                        (f"kh{p}", 0, kf_d, "OF1"),
                        (f"vhf{p}", 0, vf_d, "OF2")):
                    ap = {f"qh{p}": qh, f"kh{p}": kh, f"vhf{p}": vhf}[nmq]
                    sc.dma("sync",
                           ap[:, p * HS:(p + 1) * HS]
                           .rearrange("q (c s) -> q c s", c=HC),
                           src_d[HD * h:HD * (h + 1), :]
                           .rearrange("(c q) s -> q c s", q=128),
                           (nmq,), (nmq,), f"hld.{nmq}.{h}")

            def head_vtrans(h):
                p = h % 2
                vp = h % 2
                v_ap = vhf[:, p * HS:(p + 1) * HS]
                pairs = [(tt, hcc) for tt in range(TT) for hcc in range(HC)]
                for g in range(0, len(pairs), 4):
                    grp = pairs[g:g + 4]
                    sl = (g // 4) % 2
                    def em(v_ap=v_ap, grp=grp, sl=sl):
                        def f(e):
                            ins = None
                            for k, (tt, hcc) in enumerate(grp):
                                ins = e.transpose(
                                    pspt[:, 1024 * sl + 128 * k:
                                         1024 * sl + 128 * (k + 1)],
                                    v_ap[:, hcc * S + 128 * tt:
                                         hcc * S + 128 * (tt + 1)],
                                    id_sb[:, :])
                            return ins
                        return f
                    sc.op("tensor", em(), (f"vhf{p}", "id_sb"),
                          (f"pst_F{2 * sl}", f"pst_F{2 * sl + 1}"),
                          tag=f"vtr{h}.{g}")
                    def ev(grp=grp, sl=sl, vp=vp):
                        def f(e):
                            base = vp * TT * HD + (grp[0][0] * HC + grp[0][1]) * 128
                            return e.tensor_copy(
                                vtm[:, base:base + 128 * len(grp)],
                                pspt[:, 1024 * sl:1024 * sl + 128 * len(grp)])
                        return f
                    sc.op("vector", ev(),
                          (f"pst_F{2 * sl}", f"pst_F{2 * sl + 1}"),
                          (f"vtm{vp}",), tag=f"vev{h}.{g}")

            def head_spill(h):
                p = h % 2
                for tg in range(TC):
                    for hcc in range(HC):
                        sc.dma("sync", aout_d[tg, h * HC + hcc, :, :],
                               aoh[:, p * HS + hcc * S + 512 * tg:
                                   p * HS + hcc * S + 512 * (tg + 1)],
                               (f"aoh{p}",), ("AOUT",), f"aosp{h}.{tg}.{hcc}")

            def s1(h, qt):
                p = h % 2
                i4 = qt % 4
                sp = qt % 2
                q_ap = qh[:, p * HS:(p + 1) * HS]
                k_ap = kh[:, p * HS:(p + 1) * HS]
                nkb = qt // 4 + 1
                span = 512 * nkb
                pb = pbuf[:, i4 * S:i4 * S + S]
                sfb = sfb_t[:, sp * SW:sp * SW + SW]
                sfn = f"sfb{sp}"
                def smm(q_ap=q_ap, k_ap=k_ap, qt=qt, nkb=nkb):
                    def f(e):
                        ins = None
                        for kb in range(nkb):
                            for hcc in range(HC):
                                ins = e.matmul(
                                    pss[:, 512 * kb:512 * (kb + 1)],
                                    q_ap[:, hcc * S + 128 * qt:
                                         hcc * S + 128 * (qt + 1)],
                                    k_ap[:, hcc * S + 512 * kb:
                                         hcc * S + 512 * (kb + 1)],
                                    start=(hcc == 0), stop=(hcc == HC - 1))
                        return ins
                    return f
                sc.op("tensor", smm(), (f"qh{p}", f"kh{p}"),
                      tuple(f"pss_F{b}" for b in range(nkb)),
                      tag=f"s{h}.{qt}")
                db = nkb - 1
                sc.op("vector",
                      (lambda db=db, qt=qt, sfb=sfb: (lambda e:
                          e.scalar_tensor_tensor(
                              sfb[:, 512 * db:512 * (db + 1)],
                              pss[:, 512 * db:512 * (db + 1)], 1.0,
                              mask_sb[:, 512 * (qt % 4):512 * (qt % 4 + 1)],
                              op0=ALU.mult, op1=ALU.add)))(),
                      (f"pss_F{db}", "mask_sb"), (sfn,), tag=f"smk{h}.{qt}")
                sc.op("vector",
                      (lambda db=db, sfb=sfb: (lambda e: e.tensor_scalar(
                          sfb[:, 512 * db:512 * (db + 1)],
                          sfb[:, 512 * db:512 * (db + 1)],
                          0.0, None, op0=ALU.add, op1=ALU.max,
                          accum_out=mrun[:, :])))(),
                      (sfn,), (sfn, "mrun"), tag=f"sm0{h}.{qt}")
                for b in range(nkb - 1):
                    sc.op("vector",
                          (lambda b=b, sfb=sfb: (lambda e: e.tensor_scalar(
                              sfb[:, 512 * b:512 * (b + 1)],
                              pss[:, 512 * b:512 * (b + 1)],
                              0.0, mrun[:, :], op0=ALU.add, op1=ALU.max,
                              accum_out=mrun[:, :])))(),
                          (f"pss_F{b}", "mrun"), (sfn, "mrun"),
                          tag=f"sev{h}.{qt}.{b}")
                sc.op("vector",
                      (lambda i4=i4: (lambda e: e.tensor_scalar(
                          bias16[:, i4:i4 + 1], mrun[:, :], -cfg.scale, None,
                          op0=ALU.mult)))(),
                      ("mrun",), (f"bias16{i4}",), tag=f"b16.{h}.{qt}")
                sc.op("scalar",
                      (lambda span=span, pb=pb, i4=i4, sfb=sfb: (lambda e:
                          e.activation(
                              pb[:, 0:span], sfb[:, 0:span], AF.Exp,
                              bias=bias16[:, i4:i4 + 1], scale=cfg.scale,
                              accum_out=lsum[:, i4:i4 + 1])))(),
                      (sfn, f"bias16{i4}"), (f"pbuf{i4}", f"lsum{i4}"),
                      tag=f"ex{h}.{qt}")

            def s2a(h, qt):
                i4 = qt % 4
                tp = qt % 2
                pb = pbuf[:, i4 * S:i4 * S + S]
                nt = qt + 1
                def ptburst(pb=pb, nt=nt):
                    def f(e):
                        ins = None
                        for kt in range(nt):
                            ins = e.transpose(
                                pspt[:, 128 * kt:128 * (kt + 1)],
                                pb[:, 128 * kt:128 * (kt + 1)], id_sb[:, :])
                        return ins
                    return f
                sc.op("tensor", ptburst(), (f"pbuf{i4}", "id_sb"),
                      tuple(f"pst_F{u}" for u in range((nt + 3) // 4)),
                      tag=f"pt{h}.{qt}")
                sc.op("scalar",
                      (lambda nt=nt, tp=tp: (lambda e: e.activation(
                          ptb[:, tp * S:tp * S + 128 * nt],
                          pspt[:, 0:128 * nt], AF.Copy)))(),
                      tuple(f"pst_F{u}" for u in range((nt + 3) // 4)),
                      (f"ptb{tp}",), tag=f"pte{h}.{qt}")

            def s2b(h, qt):
                p = h % 2
                tp = qt % 2
                vp = h % 2
                def avm(qt=qt, tp=tp, vp=vp):
                    def f(e):
                        ins = None
                        for kt in range(qt + 1):
                            ins = e.matmul(
                                psav[:, 0:HD],
                                ptb[:, tp * S + 128 * kt:
                                    tp * S + 128 * (kt + 1)],
                                vtm[:, vp * TT * HD + HD * kt:
                                    vp * TT * HD + HD * (kt + 1)],
                                start=(kt == 0), stop=(kt == qt))
                        return ins
                    return f
                sc.op("tensor", avm(), (f"ptb{tp}", f"vtm{vp}"), ("psav_F",),
                      tag=f"av{h}.{qt}")

            def s2c_v(h, qt):
                i4 = qt % 4
                sc.op("vector",
                      (lambda i4=i4: (lambda e: e.reciprocal(
                          linv[:, i4:i4 + 1], lsum[:, i4:i4 + 1])))(),
                      (f"lsum{i4}",), (f"linv{i4}",), tag=f"rcp{h}.{qt}")
                sc.op("vector",
                      (lambda i4=i4: (lambda e: e.tensor_scalar(
                          avtm[:, :], psav[:, 0:HD], linv[:, i4:i4 + 1],
                          None, op0=ALU.mult)))(),
                      ("psav_F", f"linv{i4}"), ("avtm",), tag=f"ave{h}.{qt}")

            def s2c_pe(h, qt):
                p = h % 2
                def atr():
                    def f(e):
                        ins = None
                        for hcc in range(HC):
                            ins = e.transpose(
                                psvt[:, 128 * hcc:128 * (hcc + 1)],
                                avtm[:, 128 * hcc:128 * (hcc + 1)],
                                id_sb[:, :])
                        return ins
                    return f
                sc.op("tensor", atr(), ("avtm", "id_sb"), ("psvt_F",),
                      tag=f"atr{h}.{qt}")
                def aev(qt=qt, p=p):
                    def f(e):
                        return e.tensor_copy(
                            aoh[:, p * HS:(p + 1) * HS]
                            .rearrange("q (c s) -> q c s", c=HC)
                            [:, :, 128 * qt:128 * (qt + 1)],
                            psvt[:, 0:128 * HC]
                            .rearrange("q (c f) -> q c f", c=HC))
                    return f
                sc.op("vector", aev(), ("psvt_F",), (f"aoh{p}",),
                      tag=f"aev{h}.{qt}")

            flat = [(h, qt) for h in range(H) for qt in range(TT)]
            LAG_A, LAG_B, LAG_C = 2, 3, 4
            head_loads(0)
            head_vtrans(0)
            head_loads(1)
            N = len(flat)
            for fi in range(N + LAG_C):
                if fi >= LAG_C:
                    s2c_v(*flat[fi - LAG_C])
                if fi < N:
                    h, qt = flat[fi]
                    s1(h, qt)
                    if qt == LAG_B and h >= 1 and h + 1 < H:
                        head_loads(h + 1)
                    if qt == LAG_B + 2 and h + 1 < H:
                        head_vtrans(h + 1)
                if fi >= LAG_A and fi - LAG_A < N:
                    s2a(*flat[fi - LAG_A])
                if fi >= LAG_C:
                    s2c_pe(*flat[fi - LAG_C])
                if fi >= LAG_B and fi - LAG_B < N:
                    s2b(*flat[fi - LAG_B])
                if fi < N and flat[fi][1] == LAG_C and flat[fi][0] >= 1:
                    head_spill(flat[fi][0] - 1)
            head_spill(H - 1)
            sc.end_block()

        OPW = min(512, D)
        NOP = D // OPW
        with contextlib.ExitStack() as G:
            b2b_sb = G.enter_context(nc.sbuf_tensor("b2b_sb", [128, D], F32))
            wonat = G.enter_context(nc.sbuf_tensor("wonat", [128, 4 * D], BF16))
            woTq = G.enter_context(nc.sbuf_tensor("woTq", [128, DC * OPW], BF16))
            v2q = G.enter_context(nc.sbuf_tensor("v2q", [128, 2 * RC * OPW], BF16))
            aoutg = G.enter_context(nc.sbuf_tensor("aoutg", [128, 2 * DC * 512],
                                                   BF16))
            xres = G.enter_context(nc.sbuf_tensor("xres", [128, 2 * OPW], F32))
            outsb = G.enter_context(nc.sbuf_tensor("outsb", [128, 2 * OPW], F32))
            psf = G.enter_context(nc.psum_tensor("psf_G", [128, 1024], F32))
            pswt = G.enter_context(nc.psum_tensor("pswt_G", [128, 2048], BF16))

            sc.dma("sync", b2b_sb[:, :], b2b_d[:, :], (), ("b2b_sb",), "ld_b2b")
            wo_d = w_attn["wo"]

            def ao_load(op_i, tg):
                gp = tg % 2
                sc.dma("sync",
                       aoutg[:, gp * DC * 512:(gp + 1) * DC * 512]
                       .rearrange("q (n f) -> q n f", n=DC),
                       aout_d[tg, :, :, :].rearrange("n q f -> q n f"),
                       (f"aoutg{gp}",), (f"aoutg{gp}",), f"aold{op_i}.{tg}")

            NSL = OPW // 128

            def wo_load(op_i, slab):
                oc = (OPW * op_i) // 128 + slab
                p = slab % 4
                sc.dma("gpsimd", wonat[:, p * D:(p + 1) * D],
                       wo_d[128 * oc:128 * (oc + 1), :],
                       (f"wonat{p}",), (f"wonat{p}",), f"wold{op_i}.{slab}")

            def v2_load(op_i):
                vp = op_i % 2
                sc.dma("gpsimd",
                       v2q[:, vp * RC * OPW:(vp + 1) * RC * OPW]
                       .rearrange("q (n o) -> q n o", n=RC),
                       v2_d[:, OPW * op_i:OPW * (op_i + 1)]
                       .rearrange("(n q) o -> q n o", q=128),
                       (f"v2q{vp}",), (f"v2q{vp}",), f"v2ld{op_i}")

            for slab in range(NSL):
                wo_load(0, slab)
            v2_load(0)
            for op_i in range(NOP):
                for slab in range(NSL):
                    p = slab % 4
                    wo_ap = wonat[:, p * D:(p + 1) * D]
                    for g in range((DC + 3) // 4):
                        ics = list(range(4 * g, min(4 * g + 4, DC)))
                        sl = g % 2
                        def em(wo_ap=wo_ap, ics=ics, sl=sl):
                            def f(e):
                                ins = None
                                for kk, ic in enumerate(ics):
                                    ins = e.transpose(
                                        pswt[:, 1024 * sl + 128 * kk:
                                             1024 * sl + 128 * (kk + 1)],
                                        wo_ap[:, 128 * ic:128 * (ic + 1)],
                                        id_sb[:, :])
                                return ins
                            return f
                        sc.op("tensor", em(), (f"wonat{p}", "id_sb"),
                              (f"pswt_G{sl}",), tag=f"wotr{op_i}.{slab}.{g}")
                        def ev(ics=ics, sl=sl, slab=slab):
                            def f(e):
                                return e.tensor_copy(
                                    woTq[:, :].rearrange(
                                        "q (c o) -> q c o", c=DC)
                                    [:, ics[0]:ics[0] + len(ics),
                                     128 * slab:128 * (slab + 1)],
                                    pswt[:, 1024 * sl:
                                         1024 * sl + 128 * len(ics)]
                                    .rearrange("q (c f) -> q c f", c=len(ics)))
                            return f
                        sc.op("vector", ev(), (f"pswt_G{sl}",),
                              ("woTq",), tag=f"woev{op_i}.{slab}.{g}")
                if op_i + 1 < NOP:
                    for slab in range(NSL):
                        wo_load(op_i + 1, slab)
                    v2_load(op_i + 1)
                if op_i == 0:
                    ao_load(0, 0)
                for tt in range(TT):
                    tg = tt // 4
                    gp = tg % 2
                    if tt % 4 == 1 and tg + 1 < TC:
                        ao_load(op_i, tg + 1)
                    p = tt % 2
                    xr_ap = xres[:, p * OPW:(p + 1) * OPW]
                    ou_ap = outsb[:, p * OPW:(p + 1) * OPW]
                    sc.dma("sync", xr_ap,
                           x_d[128 * tt:128 * (tt + 1),
                               OPW * op_i:OPW * (op_i + 1)],
                           (f"xres{p}",), (f"xres{p}",), f"xrl{op_i}.{tt}")
                    sc.op("gpsimd",
                          (lambda xr_ap=xr_ap, op_i=op_i: (lambda e:
                              e.tensor_tensor(
                                  xr_ap, xr_ap,
                                  b2b_sb[:, OPW * op_i:OPW * (op_i + 1)],
                                  ALU.add)))(),
                          (f"xres{p}", "b2b_sb"), (f"xres{p}",),
                          tag=f"xb2.{op_i}.{tt}")
                    def fmm(tt=tt, p=p, gp=gp, op_i=op_i):
                        def f(e):
                            ins = None
                            for i in range(DC):
                                ins = e.matmul(
                                    psf[:, 512 * p:512 * (p + 1)],
                                    aoutg[:, gp * DC * 512 + 512 * i
                                          + 128 * (tt % 4):
                                          gp * DC * 512 + 512 * i
                                          + 128 * (tt % 4 + 1)],
                                    woTq[:, OPW * i:OPW * i + 512],
                                    start=(i == 0), stop=False)
                            vp = op_i % 2
                            for r in range(RC):
                                ins = e.matmul(
                                    psf[:, 512 * p:512 * (p + 1)],
                                    zbuf[:, r * S + 128 * tt:
                                         r * S + 128 * (tt + 1)],
                                    v2q[:, vp * RC * OPW + OPW * r:
                                        vp * RC * OPW + OPW * r + 512],
                                    start=False, stop=(r == RC - 1))
                            return ins
                        return f
                    sc.op("tensor", fmm(), (f"aoutg{gp}", "woTq", "zbuf",
                                            f"v2q{op_i % 2}"),
                          (f"psf_G{p}",), tag=f"fmm{op_i}.{tt}")
                    sc.op("vector",
                          (lambda xr_ap=xr_ap, ou_ap=ou_ap, p=p: (lambda e:
                              e.scalar_tensor_tensor(
                                  ou_ap, xr_ap, 1.0,
                                  psf[:, 512 * p:512 * (p + 1)],
                                  op0=ALU.mult, op1=ALU.add)))(),
                          (f"xres{p}", f"psf_G{p}"), (f"outsb{p}",),
                          tag=f"oev{op_i}.{tt}")
                    sc.dma("sync",
                           out_d[128 * tt:128 * (tt + 1),
                                 OPW * op_i:OPW * (op_i + 1)], ou_ap,
                           (f"outsb{p}",), ("OUT",), f"osp{op_i}.{tt}")
                if op_i + 1 < NOP:
                    ao_load(op_i + 1, 0)
            sc.end_block()
        st_z.close()

    return nc



def make_inmaps(cfg, roles, x_full):
    consts = make_consts(cfg)
    b1 = np.asarray(roles["b1"], dtype=np.float32)
    b2 = np.asarray(roles["b2"], dtype=np.float32)
    base = {k: np.ascontiguousarray(np.asarray(roles[k], dtype=np.float32))
            for k in ("wq", "wk", "wv", "wo", "u1", "v1", "u2", "v2")}
    base["b1pt"] = np.ascontiguousarray(b1.reshape(2 * cfg.IC, 128).T)
    base["b2b"] = np.ascontiguousarray(
        np.broadcast_to(b2, (128, cfg.D)).astype(np.float32))
    base.update({k: np.ascontiguousarray(v) for k, v in consts.items()})
    base["epsv"] = np.full((128, 1), cfg.eps, dtype=np.float32)
    return [dict(base, x=np.ascontiguousarray(x_full[b], dtype=np.float32))
            for b in range(x_full.shape[0])]


def kernel(L0_attn_hidden_states, L0_attn_wq, L0_attn_wk, L0_attn_wv,
           L0_attn_wo, L0_ffn_U1, L0_ffn_V1, L0_ffn_U2, L0_ffn_V2,
           L0_ffn_b1, L0_ffn_b2):
    cfg = CFG_FULL
    x = np.asarray(L0_attn_hidden_states, dtype=np.float32)
    B = x.shape[0]
    roles = {"wq": L0_attn_wq, "wk": L0_attn_wk, "wv": L0_attn_wv,
             "wo": L0_attn_wo, "u1": L0_ffn_U1, "v1": L0_ffn_V1,
             "u2": L0_ffn_U2, "v2": L0_ffn_V2, "b1": L0_ffn_b1,
             "b2": L0_ffn_b2}
    nc = build_nc(cfg)
    in_maps = make_inmaps(cfg, roles, x)
    res = run_bass_kernel_spmd(nc, in_maps, core_ids=list(range(B)))
    return np.stack([res.results[b]["out"] for b in range(B)], axis=0)
